# version 03 (final)
# baseline (speedup 1.0000x reference)
import numpy as np
import ml_dtypes
import concourse.bass as bass
import concourse.mybir as mybir
from concourse.bass_utils import run_bass_kernel_spmd

F32 = mybir.dt.float32
BF16 = mybir.dt.bfloat16
AF = mybir.ActivationFunctionType
ALU = mybir.AluOpType
AX = mybir.AxisListType

BF = ml_dtypes.bfloat16
NCORES = 8
B = 2
T = 8192
D = 1024
NH = 16
HD = 64
NJ = 8
NTILE = 16
DHID = 4096
HID = 2730
EPS = 1e-5

GAMMA = 1.0 - np.exp2(-5.0 - np.arange(NH, dtype=np.float64))

ENGS = ("sync", "gpsimd", "vector", "scalar", "tensor")



def _row(a):
    return np.ascontiguousarray(np.asarray(a, dtype=np.float32).reshape(1, -1))


def _chunk_cols(a, nchunk):
    a = np.asarray(a, dtype=np.float32).ravel()
    pad = nchunk * 128 - a.size
    if pad:
        a = np.concatenate([a, np.zeros(pad, np.float32)])
    return np.ascontiguousarray(a.reshape(nchunk, 128).T)


def stage_shared(inp):
    s = {}
    t32 = lambda a: np.ascontiguousarray(np.asarray(a, dtype=np.float32).T)
    c32 = lambda a: np.ascontiguousarray(np.asarray(a, dtype=np.float32))
    wa = np.asarray(inp["L0_attn_w_attn"], np.float32)
    s["wqT"] = t32(wa[0:D])
    s["wkT"] = t32(wa[D:2 * D])
    s["wvT"] = t32(wa[2 * D:3 * D])
    ba = np.asarray(inp["L0_attn_b_attn"], np.float32)
    s["bq_c"] = _chunk_cols(ba[0:D], 8)
    s["bk_c"] = _chunk_cols(ba[D:2 * D], 8)
    s["bv_r"] = _row(ba[2 * D:3 * D])
    s["wprojT"] = t32(inp["L0_attn_w_proj"])
    s["bproj_r"] = _row(inp["L0_attn_b_proj"])
    s["w1T"] = t32(inp["L0_ffn_w1_w"])
    s["w2T"] = t32(inp["L0_ffn_w2_w"])
    s["h1T"] = t32(inp["L0_ffn_h1_w"])
    s["h2T"] = t32(inp["L0_ffn_h2_w"])
    s["w3T"] = t32(inp["L0_ffn_w3_w"])
    s["w1s_c"] = _chunk_cols(inp["L0_ffn_w1_s"], 32)
    s["w2s_c"] = _chunk_cols(inp["L0_ffn_w2_s"], 32)
    s["h1s_c"] = _chunk_cols(inp["L0_ffn_h1_s"], 32)
    s["h2s_c"] = _chunk_cols(inp["L0_ffn_h2_s"], 32)
    s["w3s_r"] = _row(inp["L0_ffn_w3_s"])
    s["ln1w_r"] = _row(inp["L1_attn_ln1_w"])
    s["ln1b_r"] = _row(inp["L1_attn_ln1_b"])
    s["wqr"] = c32(inp["L1_attn_wq"])
    s["wkr"] = c32(inp["L1_attn_wk"])
    s["wvr"] = c32(inp["L1_attn_wv"])
    s["wgr"] = c32(inp["L1_attn_wg"])
    s["wor"] = c32(inp["L1_attn_wo"])
    s["ln2w_r"] = _row(inp["L1_attn_ln2_w"])
    s["ln2b_r"] = _row(inp["L1_attn_ln2_b"])
    win = np.asarray(inp["L1_attn_w_in"], np.float32)
    wa_ = np.zeros((D, 2816), np.float32)
    wa_[:, 0:HID] = win[:, 0:HID]
    wb_ = np.zeros((D, 2816), np.float32)
    wb_[:, 0:HID] = win[:, HID:2 * HID]
    s["winA"] = np.ascontiguousarray(wa_)
    s["winB"] = np.ascontiguousarray(wb_)
    bin_ = np.asarray(inp["L1_attn_b_in"], np.float32)
    s["binA_c"] = _chunk_cols(bin_[0:HID], 22)
    s["binB_c"] = _chunk_cols(bin_[HID:2 * HID], 22)
    wo_ = np.zeros((2816, D), np.float32)
    wo_[0:HID] = np.asarray(inp["L1_attn_w_out"], np.float32)
    s["wout"] = np.ascontiguousarray(wo_)
    s["bout_r"] = _row(inp["L1_attn_b_out"])
    s["WqkvT"] = t32(inp["L1_ffn_Wqkv"])
    s["bqkv_c"] = _chunk_cols(inp["L1_ffn_bqkv"], 24)
    s["WoT"] = t32(inp["L1_ffn_Wo"])
    s["bo_r"] = _row(inp["L1_ffn_bo"])
    s["g1_r"] = _row(inp["L1_ffn_g1"])
    s["be1_r"] = _row(inp["L1_ffn_be1"])
    s["g2_r"] = _row(inp["L1_ffn_g2"])
    s["be2_r"] = _row(inp["L1_ffn_be2"])
    s["WgT"] = t32(inp["L1_ffn_Wg"])
    s["W1"] = c32(inp["L1_ffn_W1"])
    s["b1_c"] = np.ascontiguousarray(
        np.asarray(inp["L1_ffn_b1"], np.float32).reshape(8, 32, 128).transpose(0, 2, 1))
    s["W2"] = c32(inp["L1_ffn_W2"])
    s["b2_r"] = c32(np.asarray(inp["L1_ffn_b2"], np.float32).reshape(8, D))
    return s


def stage_core(c):
    s = {}
    u = np.arange(128)
    w = np.arange(128)
    wc = np.arange(512)
    am0 = np.where(wc[None, :] <= (128 * c + u[:, None]), 0.0, -1e30).astype(np.float32)
    am1 = np.where(512 + wc[None, :] <= (128 * c + u[:, None]), 0.0, -1e30).astype(np.float32)
    s["am01"] = np.ascontiguousarray(np.stack([am0, am1]))
    m = np.zeros((8, 128, 128), np.float32)
    for rho in range(8):
        m[rho] = ((rho - c) * 128 + w[:, None] <= u[None, :]).astype(np.float32)
    s["m01"] = np.ascontiguousarray(m.astype(BF))
    qp = np.zeros((8, 4, 128, 128), np.float64)
    kp = np.zeros((8, 128, 128), np.float64)
    for ct in range(8):
        for half in range(2):
            h = 2 * ct + half
            g = GAMMA[h]
            for jm in range(4):
                jme = 0 if h == 0 else 1024 * jm
                qp[ct, jm, 64 * half:64 * half + 64, :] = (
                    0.125 * np.power(g, u[None, :] + 128 * c + jme))
            kp[ct, 64 * half:64 * half + 64, :] = np.power(g, -u[None, :].astype(np.float64))
    s["qpat"] = np.ascontiguousarray(qp.astype(np.float32))
    s["kpat"] = np.ascontiguousarray(kp.astype(np.float32))
    s["qpat2"] = np.ascontiguousarray(qp.transpose(0, 2, 1, 3).reshape(1024, 512).astype(np.float32))
    s["kpat2"] = np.ascontiguousarray(
        np.tile(kp[:, :, None, :], (1, 1, 4, 1)).reshape(8, 128, 512).reshape(1024, 512).astype(np.float32))
    return s


def stage_x(x, c):
    xr = np.ascontiguousarray(np.asarray(x, np.float32).reshape(B, 8, NCORES, 128, D)[:, :, c])
    return xr.reshape(NTILE, 128, D)


def unstage_y(parts):
    out = np.empty((B, 8, NCORES, 128, D), np.float32)
    for c in range(NCORES):
        out[:, :, c] = parts[c].reshape(B, NJ, 128, D)
    return np.ascontiguousarray(out.reshape(B, T, D))


def to_bf(x):
    return np.ascontiguousarray(np.asarray(x).astype(BF))



class Prog:
    def __init__(self, name, fused=False):
        self.nc = bass.Bass("TRN2", target_bir_lowering=False, name=name)
        self.q = {e: [] for e in ENGS}
        self.sems = {}
        self.input_keys = {}
        self.nbar = 0
        self.fused = fused
        self.prefix = ""
        self.io_tensors = {}
        self._guards = []
        self._consts_done = False
        self._last_bc = None
        self._pool = {"sw": [], "hw": []}
        self._phase_names = []

    def phase_start(self, prefix):
        self.prefix = prefix
        self._guards = []
        self._last_bc = None

    def phase_end(self):
        for g in reversed(self._guards):
            g.__exit__(None, None, None)
        self._guards = []
        self.prefix = ""
        self.barrier()
        for full in self._phase_names:
            ent = self.sems[full]
            if ent["h"] is not None:
                self._pool[ent["cls"]].append((ent["h"], ent["base"] + ent["n"]))
        self._phase_names = []
        self._last_bc = None

    def _is_global(self, name):
        return name.startswith("__") or name == "cld"

    def semh(self, name):
        full = name if (self._is_global(name) or not self.fused) else self.prefix + name
        if full not in self.sems:
            if self.fused and not self._is_global(name):
                self.sems[full] = {"h": None, "base": 0, "n": 0, "cls": None}
                self._phase_names.append(full)
            else:
                self.sems[full] = {"h": self.nc.alloc_semaphore(full), "base": 0,
                                   "n": 0, "cls": "hw"}
        return self.sems[full]

    def cnt(self, name):
        return self.semh(name)["n"]

    def wait(self, eng, name, val=None):
        ent = self.semh(name)
        if val is None:
            nrec = ent["n"]
            self.q[eng].append(
                lambda e, ent=ent, nrec=nrec: (e.wait_ge(ent["h"], ent["base"] + nrec)
                                               if ent["h"] is not None else None))
        else:
            self.q[eng].append(lambda e, ent=ent, v=val: e.wait_ge(ent["h"], ent["base"] + v))

    def call(self, eng, meth, *args, inc=None, n=1, **kw):
        ent = None
        if inc is not None:
            ent = self.semh(inc)
            if ent["h"] is None:
                cls = "sw" if (meth == "dma_start" and eng == "gpsimd") else "hw"
                if self._pool[cls]:
                    h, inherited = self._pool[cls].pop()
                else:
                    h, inherited = self.nc.alloc_semaphore(self.prefix + inc), 0
                ent.update(h=h, base=inherited, cls=cls)
            ent["n"] += n

        def thunk(e, meth=meth, args=args, kw=kw, ent=ent, n=n):
            inst = getattr(e, meth)(*args, **kw)
            if ent is not None:
                inst.then_inc(ent["h"], n)
        self.q[eng].append(thunk)

    def dma(self, eng, out, in_, inc, **kw):
        self.call(eng, "dma_start", out=out, in_=in_, inc=inc, n=16, **kw)

    def barrier(self):
        self.nbar += 1
        ent = self.semh("__bar")
        ent["n"] += len(ENGS)
        v = ent["n"]
        h = ent["h"]
        for e in ENGS:
            self.q[e].append(lambda en, h=h: en.sem_inc(h, 1))
            self.q[e].append(lambda en, h=h, v=v: en.wait_ge(h, v))

    def din(self, name, shape, dtype, key):
        if self.fused and name in self.io_tensors:
            return self.io_tensors[name]
        self.input_keys[name] = key
        t = self.nc.dram_tensor(name, list(shape), dtype, kind="ExternalInput")
        if self.fused:
            self.io_tensors[name] = t
        return t

    def dout(self, name, shape, dtype, final=False):
        if name in getattr(self, "debug_finals", ()):
            final = True
        if self.fused and not final:
            t = self.nc.dram_tensor(name, list(shape), dtype, kind="Internal")
            self.io_tensors[name] = t
            return t
        return self.nc.dram_tensor(name, list(shape), dtype, kind="ExternalOutput")

    def dint(self, name, shape, dtype):
        return self.nc.dram_tensor(name, list(shape), dtype, kind="Internal")

    def sb(self, name, shape, dtype):
        if not self.fused:
            return self.nc.alloc_sbuf_tensor(self.prefix + name, list(shape), dtype)
        g = self.nc.sbuf_tensor(self.prefix + name, list(shape), dtype)
        h = g.__enter__()
        self._guards.append(g)
        return h

    def ps(self, name, shape, dtype=F32):
        if not self.fused:
            return self.nc.alloc_psum_tensor(self.prefix + name, list(shape), dtype)
        g = self.nc.psum_tensor(self.prefix + name, list(shape), dtype)
        h = g.__enter__()
        self._guards.append(g)
        return h

    def init_consts(self):
        if self._consts_done:
            return self.ident, self.ones1
        self._consts_done = True
        pfx, self.prefix = self.prefix, ""
        gsave, self._guards = self._guards, []
        self.ident = self.sb("identsb", [128, 128], BF16)
        self.ones1 = self.sb("ones1sb", [1, 128], BF16)
        self.eps_col = self.sb("epscol", [128, 1], F32)
        self._guards = []
        self.prefix = pfx
        self._guards_dummy = gsave
        self._guards = gsave
        it = self.nc.inline_tensor(np.eye(128, dtype=BF), "identc")
        ot = self.nc.inline_tensor(np.ones((1, 128), dtype=BF), "onesc")
        et = self.nc.inline_tensor(np.full((128, 1), EPS, np.float32), "epsc")
        self.dma("sync", self.ident[:, :], it.ap(), "cld")
        self.dma("sync", self.eps_col[:, :], et.ap(), "cld")
        self.dma("sync", self.ones1[:, :], ot.ap(), "cld")
        self._cld_total = 48
        return self.ident, self.ones1

    def build(self):
        with self.nc.Block() as block:
            for ename in ENGS:
                thunks = self.q[ename]
                getattr(block, ename)(lambda e, thunks=thunks: [t(e) for t in thunks])
        return self.nc

    def run(self, in_maps, **kw):
        nc = self.build()
        return run_bass_kernel_spmd(nc, in_maps, core_ids=list(range(NCORES)), **kw)



def emit_ln_prep(p, pre, x_ap, xnT, ident, keep_tm=None, affine=None,
                 xt_ext=None, tm_dma=None, xnT_dma=None):
    eps_col = p.eps_col
    """Token-major LayerNorm over 16 tiles + bf16 + PE transpose to xnT.

    x_ap: DRAM AP [16, 128, 1024] f32 (per-tile source), or None with
      xt_ext = list of 16 SBUF [128,1024] f32 APs already resident.
    xnT: SBUF [128, 16384] bf16 (cols ct*2048 + local_token).
    affine: (w_bcast, b_bcast) SBUF [128, 1024] f32 tiles (gate sem
      pre+"afr" must be bumped by caller before recording this).
    keep_tm: SBUF [128, 16384] bf16 to also keep the (affined) token-major.
    Completion: sem pre+"xnev" == 16 (and pre+"ktm" == 16 if keep_tm).
    """
    S = lambda n: pre + n
    xt = [p.sb(S("xt0"), [128, 1024], F32), p.sb(S("xt1"), [128, 1024], F32)] if x_ap is not None else None
    xsq = [p.sb(S("xsq0"), [128, 1024], F32), p.sb(S("xsq1"), [128, 1024], F32)]
    st = p.sb(S("st"), [128, 16], F32)
    xnb = [p.sb(S("xnb0"), [128, 1024], BF16), p.sb(S("xnb1"), [128, 1024], BF16)]
    tp = [p.ps(S("tp0"), [128, 1024], BF16), p.ps(S("tp1"), [128, 1024], BF16)]
    stg = [p.sb(S("stg0"), [128, 1024], BF16), p.sb(S("stg1"), [128, 1024], BF16)] if xnT_dma is not None else None

    for i in range(NTILE):
        b = i % 2
        c0 = 8 * b
        if i == 0:
            p.wait("scalar", "cld")
            p.wait("vector", "cld")
            p.wait("tensor", "cld")
        sum_, ssq = st[:, c0:c0 + 1], st[:, c0 + 1:c0 + 2]
        mu, mu2 = st[:, c0 + 2:c0 + 3], st[:, c0 + 3:c0 + 4]
        var, sd = st[:, c0 + 4:c0 + 5], st[:, c0 + 5:c0 + 6]
        rstd, nmr = st[:, c0 + 6:c0 + 7], st[:, c0 + 7:c0 + 8]

        if x_ap is not None:
            if i >= 2:
                p.wait("sync", S("xnr"), i - 1)
            p.dma("sync", xt[b][:, :], x_ap[i], S("xld" + "AB"[b]))
            src = xt[b][:, :]
        else:
            src = xt_ext[i]

        if x_ap is not None:
            p.wait("scalar", S("xld" + "AB"[b]))
        if i >= 2:
            p.wait("scalar", S("ssqr"), i - 1)
            if affine is not None:
                p.wait("scalar", S("afd"), i - 1)
        p.call("scalar", "activation", xsq[b][:, :], src, AF.Square, inc=S("sqd"))

        if x_ap is not None:
            p.wait("vector", S("xld" + "AB"[b]))
        if i >= 2:
            p.wait("vector", S("xnr"), i - 1)
        p.call("vector", "reduce_sum", sum_, src, axis=AX.X, inc=S("vch"))
        p.wait("vector", S("sqd"))
        p.call("vector", "reduce_sum", ssq, xsq[b][:, :], axis=AX.X, inc=S("ssqr"))
        p.wait("vector", S("vch"))
        p.call("vector", "tensor_scalar", mu, sum_, 1.0 / D, None, ALU.mult, inc=S("vch"))
        p.wait("vector", S("vch"))
        p.call("vector", "tensor_mul", mu2, mu, mu, inc=S("vch"))
        p.wait("vector", S("vch"))
        p.wait("vector", S("ssqr"))
        p.call("vector", "scalar_tensor_tensor", var, ssq, 1.0 / D, mu2, ALU.mult, ALU.subtract, inc=S("varr"))

        p.wait("scalar", S("varr"))
        p.call("scalar", "activation", sd, var, AF.Sqrt, bias=eps_col[:, 0:1], inc=S("sdr"))

        p.wait("vector", S("sdr"))
        p.call("vector", "reciprocal", rstd, sd, inc=S("vch"))
        p.wait("vector", S("vch"))
        if i >= 2:
            p.wait("vector", S("tpd"), i - 1)
        if tm_dma is not None and i >= 2:
            p.wait("vector", S("ktm" + "AB"[b]), 16 * (i // 2))
        if affine is None:
            p.call("vector", "tensor_scalar", xnb[b][:, :], src, mu, rstd, ALU.subtract, ALU.mult, inc=S("xnr"))
        else:
            p.call("vector", "tensor_scalar", xsq[b][:, :], src, mu, rstd, ALU.subtract, ALU.mult, inc=S("xnr"))
            wbc, bbc, aff_gate = affine
            p.wait("vector", S("xnr"))
            if i == 0:
                p.wait("vector", aff_gate)
            p.call("vector", "tensor_mul", xsq[b][:, :], xsq[b][:, :], wbc[:, :], inc=S("vch"))
            p.wait("vector", S("vch"))
            p.call("vector", "tensor_add", xnb[b][:, :], xsq[b][:, :], bbc[:, :], inc=S("afd"))

        p.wait("tensor", S("afd" if affine is not None else "xnr"))
        if i >= 2:
            p.wait("tensor", S("xnev"), i - 1)
        for k in range(8):
            p.call("tensor", "transpose", tp[b][:, k * 128:(k + 1) * 128],
                   xnb[b][:, k * 128:(k + 1) * 128], ident[:, :],
                   inc=(S("tpd") if k == 7 else None))

        p.wait("vector", S("tpd"))
        if xnT_dma is None:
            dst = xnT[:, :].rearrange("p (a c) -> p a c", a=8)[:, :, i * 128:(i + 1) * 128]
            p.call("vector", "tensor_copy", dst, tp[b][:, :].rearrange("p (a c) -> p a c", a=8), inc=S("xnev"))
        else:
            if i >= 2:
                p.wait("vector", S("xnst" + "AB"[b]), 16 * (i // 2))
            p.call("vector", "tensor_copy", stg[b][:, :], tp[b][:, :], inc=S("xnev"))
            p.wait("sync", S("xnev"), p.cnt(S("xnev")))
            ddst = xnT_dma.rearrange("p (a c) -> p a c", a=8)[:, :, i * 128:(i + 1) * 128]
            p.dma("sync", ddst, stg[b][:, :].rearrange("p (a c) -> p a c", a=8), S("xnst" + "AB"[b]))
        if keep_tm is not None:
            p.call("vector", "tensor_copy", keep_tm[:, i * 1024:(i + 1) * 1024], xnb[b][:, :], inc=S("ktm"))
        if tm_dma is not None:
            p.wait("sync", S("afd" if affine is not None else "xnr"), p.cnt(S("afd" if affine is not None else "xnr")))
            p.dma("sync", tm_dma[:, i * 1024:(i + 1) * 1024], xnb[b][:, :], S("ktm" + "AB"[b]))


def emit_bcast_row(p, pre, row_sb_f32, dst_f32, ones1_bf, ncols, pb, row_bf=None):
    S = lambda n: pre + n
    if row_bf is not None:
        rb = row_bf
        p.call("vector", "nop", inc=S("rbf"))
    else:
        rb = p.sb(S("rowbf"), [1, ncols], BF16)
        p.call("vector", "tensor_copy", rb[:, :], row_sb_f32, inc=S("rbf"))
    nch = (ncols + 511) // 512
    for kch in range(nch):
        cs, ce = kch * 512, min(ncols, kch * 512 + 512)
        p.wait("tensor", S("rbf"), p.cnt(S("rbf")))
        if getattr(p, "_last_bc", None) is not None:
            p.wait("tensor", p._last_bc[0], p._last_bc[1])
        p.call("tensor", "matmul", pb[:, 0:ce - cs], ones1_bf[:, :], rb[:, cs:ce],
               start=True, stop=True, inc=S("bcm"))
        p.wait("vector", S("bcm"))
        p.call("vector", "tensor_copy", dst_f32[:, cs:ce], pb[:, 0:ce - cs], inc=S("bcr"))
        p._last_bc = (S("bcr"), p.cnt(S("bcr")))


def emit_mm_grid(p, pre, banks, n_chains, pairs_fn, evac_fn, gate_fn=None):
    S = lambda n: pre + n
    nb = len(banks)
    for c in range(n_chains):
        bank = banks[c % nb]
        if gate_fn is not None:
            gate_fn(c)
        if c >= nb:
            p.wait("tensor", S("ev"), c - nb + 1)
        pairs = pairs_fn(c)
        for k, (l, r) in enumerate(pairs):
            p.call("tensor", "matmul", bank[:, :], l, r, start=(k == 0),
                   stop=(k == len(pairs) - 1), inc=(S("mm") if k == len(pairs) - 1 else None))
        evac_fn(c, bank)


def build_LA(core_masks, p=None):
    if p is None:
        p = Prog("LA")
    x0 = p.din("x0s", [NTILE, 128, D], F32, "x0s").ap()
    wq_d = p.din("wqT", [D, D], F32, "wqT").ap()
    wk_d = p.din("wkT", [D, D], F32, "wkT").ap()
    wv_d = p.din("wvT", [D, D], F32, "wvT").ap()
    bq_d = p.din("bq_c", [128, 8], F32, "bq_c").ap()
    bk_d = p.din("bk_c", [128, 8], F32, "bk_c").ap()
    bv_d = p.din("bv_r", [1, D], F32, "bv_r").ap()

    q0T_d = p.dout("q0T", [128, 16384], BF16).ap()
    xn0T_d = p.dout("xn0T", [128, 16384], BF16).ap()
    kv0_d = p.dout("kv0c", [4096, D], BF16).ap()

    ident, ones1 = p.init_consts()

    xnT = p.sb("xnT", [128, 16384], BF16)
    qT = p.sb("qT", [128, 16384], BF16)
    kT = p.sb("kT", [128, 16384], BF16)
    vsb = p.sb("vsb", [128, 16384], BF16)
    wq_sb = p.sb("wq_sb", [128, 8192], BF16)
    wk_sb = p.sb("wk_sb", [128, 8192], BF16)
    wv_sb = p.sb("wv_sb", [128, 8192], BF16)
    bq_sb = p.sb("bq_sb", [128, 8], F32)
    bk_sb = p.sb("bk_sb", [128, 8], F32)
    bq8 = p.sb("bq8", [128, 8], F32)
    bv_row = p.sb("bv_row", [1, D], F32)
    bvb = p.sb("bvb", [128, D], F32)
    mmb = [p.ps("mmb0", [128, 512]), p.ps("mmb1", [128, 512])]
    pbc = p.ps("pbc", [128, 512])

    for nm, (wd, wsb) in (("wldq", (wq_d, wq_sb)), ("wldk", (wk_d, wk_sb)), ("wldv", (wv_d, wv_sb))):
        p.dma("gpsimd", wsb[:, :].rearrange("p (a d) -> p a d", a=8),
              wd.rearrange("(a p) d -> p a d", p=128), nm)
    p.dma("sync", bq_sb[:, :], bq_d, "cld")
    p.dma("sync", bk_sb[:, :], bk_d, "cld")
    p.dma("sync", bv_row[:, :], bv_d, "cld")
    p.wait("vector", "cld")
    p.call("vector", "tensor_scalar", bq8[:, :], bq_sb[:, :], 0.125, None, ALU.mult, inc="bqr")
    emit_bcast_row(p, "bv", bv_row[:, :], bvb, ones1, D, pbc)

    emit_ln_prep(p, "p0", x0, xnT, ident)

    def qk_grid(pre, wsb, dstT, scale, bias_sb, gate_extra):
        def pairs(c):
            m, tc = divmod(c, 4)
            return [(wsb[:, ct * 1024 + m * 128: ct * 1024 + (m + 1) * 128],
                     xnT[:, ct * 2048 + tc * 512: ct * 2048 + tc * 512 + 512])
                    for ct in range(8)]

        def gate(c):
            if c == 0:
                p.wait("tensor", "p0xnev", 16)
                p.wait("tensor", gate_extra, 16)

        def evac(c, bank):
            m, tc = divmod(c, 4)
            p.wait("vector", pre + "mm")
            if c == 0:
                p.wait("vector", "bqr", 1)
            if scale != 1.0:
                p.call("vector", "tensor_scalar", dstT[:, m * 2048 + tc * 512: m * 2048 + tc * 512 + 512],
                       bank[:, :], scale, bias_sb[:, m:m + 1], ALU.mult, ALU.add, inc=pre + "ev")
            else:
                p.call("vector", "tensor_scalar", dstT[:, m * 2048 + tc * 512: m * 2048 + tc * 512 + 512],
                       bank[:, :], bias_sb[:, m:m + 1], None, ALU.add, inc=pre + "ev")
        emit_mm_grid(p, pre, mmb, 32, pairs, evac, gate)

    qk_grid("q_", wq_sb, qT, 0.125, bq8, "wldq")
    qk_grid("k_", wk_sb, kT, 1.0, bk_sb, "wldk")

    def v_pairs(c):
        t16, half = divmod(c, 2)
        return [(xnT[:, ct * 2048 + t16 * 128: ct * 2048 + (t16 + 1) * 128],
                 wv_sb[:, ct * 1024 + half * 512: ct * 1024 + half * 512 + 512])
                for ct in range(8)]

    def v_gate(c):
        if c == 0:
            p.wait("tensor", "wldv", 16)

    def v_evac(c, bank):
        t16, half = divmod(c, 2)
        p.wait("vector", "v_mm")
        if c == 0:
            p.wait("vector", "bvbcr", p.cnt("bvbcr"))
        p.call("vector", "tensor_add", vsb[:, t16 * 1024 + half * 512: t16 * 1024 + half * 512 + 512],
               bank[:, :], bvb[:, half * 512: half * 512 + 512], inc="v_ev")
    emit_mm_grid(p, "v_", mmb, 32, v_pairs, v_evac, v_gate)

    p.wait("sync", "q_ev", 32)
    p.dma("sync", q0T_d, qT[:, :], "sto")
    p.wait("sync", "p0xnev", 16)
    p.dma("sync", xn0T_d, xnT[:, :], "sto")
    p.wait("sync", "k_ev", 32)
    for ct in range(8):
        p.dma("sync", kv0_d.rearrange("(a p x) d -> a p x d", p=128, x=2)[ct],
              kT[:, ct * 2048:(ct + 1) * 2048].rearrange("p (x d) -> p x d", x=2), "sto")
    p.wait("sync", "v_ev", 32)
    for t16 in range(16):
        p.dma("sync", kv0_d[2048 + t16 * 128: 2048 + (t16 + 1) * 128, :],
              vsb[:, t16 * 1024:(t16 + 1) * 1024], "sto")
    p.wait("sync", "sto")
    return p


def build_LB(p=None):
    if p is None:
        p = Prog("LB")
    xnT_d = p.din("xn0T", [128, 16384], BF16, "_xn0T").ap()
    w1_d = p.din("w1T", [D, DHID], F32, "w1T").ap()
    w2_d = p.din("w2T", [D, DHID], F32, "w2T").ap()
    h1_d = p.din("h1T", [DHID, DHID], F32, "h1T").ap()
    h2_d = p.din("h2T", [DHID, DHID], F32, "h2T").ap()
    w3_d = p.din("w3T", [DHID, D], F32, "w3T").ap()
    s1_d = p.din("w1s_c", [128, 32], F32, "w1s_c").ap()
    s2_d = p.din("w2s_c", [128, 32], F32, "w2s_c").ap()
    sh1_d = p.din("h1s_c", [128, 32], F32, "h1s_c").ap()
    sh2_d = p.din("h2s_c", [128, 32], F32, "h2s_c").ap()
    s3_d = p.din("w3s_r", [1, D], F32, "w3s_r").ap()
    ffn_d = p.dout("ffn0", [128, 16384], BF16).ap()

    ident, ones1 = p.init_consts()
    xnT = p.sb("xnT", [128, 16384], BF16)
    h_sb = p.sb("h_sb", [128, 16384], BF16)
    g_sb = p.sb("g_sb", [128, 16384], BF16)
    ffn_sb = p.sb("ffn_sb", [128, 16384], BF16)
    scl = p.sb("scl", [128, 128], F32)
    w3s_row = p.sb("w3s_row", [1, D], F32)
    w3sb = p.sb("w3sb", [128, D], F32)
    hsil = [p.sb("hsil0", [128, 512], BF16), p.sb("hsil1", [128, 512], BF16)]
    w1b = [p.sb("w1b0", [128, 1024], BF16), p.sb("w1b1", [128, 1024], BF16)]
    w2b = [p.sb("w2b0", [128, 1024], BF16), p.sb("w2b1", [128, 1024], BF16)]
    hb = [p.sb("hb0", [128, 4096], BF16), p.sb("hb1", [128, 4096], BF16)]
    w3b = p.sb("w3b", [128, 32 * 512], BF16)
    psA = p.ps("psA", [128, 512])
    psB = p.ps("psB", [128, 512])
    psC = [p.ps("psC0", [128, 512]), p.ps("psC1", [128, 512])]
    pbc = p.ps("pbc", [128, 512])

    p.dma("sync", xnT[:, :], xnT_d, "cld")
    p.dma("sync", scl[:, 0:32], s1_d, "cld")
    p.dma("sync", scl[:, 32:64], s2_d, "cld")
    p.dma("sync", scl[:, 64:96], sh1_d, "cld")
    p.dma("sync", scl[:, 96:128], sh2_d, "cld")
    p.dma("sync", w3s_row[:, :], s3_d, "cld")
    p.wait("vector", "cld")
    emit_bcast_row(p, "w3s", w3s_row[:, :], w3sb, ones1, D, pbc)

    cldv = p.cnt("cld")
    for tt in range(4):
        for o in range(32):
            ob = o % 2
            idx = tt * 32 + o
            if idx >= 2:
                p.wait("gpsimd", "f1mm", idx - 1)
                p.wait("gpsimd", "f2mm", idx - 1)
            p.dma("gpsimd", w1b[ob][:, :].rearrange("p (a d) -> p a d", a=8),
                  w1_d.rearrange("(a p) d -> p a d", p=128)[:, :, o * 128:(o + 1) * 128], "wld1" + "AB"[ob])
            p.dma("gpsimd", w2b[ob][:, :].rearrange("p (a d) -> p a d", a=8),
                  w2_d.rearrange("(a p) d -> p a d", p=128)[:, :, o * 128:(o + 1) * 128], "wld2" + "AB"[ob])
            p.wait("tensor", "wld1" + "AB"[ob])
            if idx == 0:
                p.wait("tensor", "cld", cldv)
            if idx >= 1:
                p.wait("tensor", "silev", idx)
            for ct in range(8):
                p.call("tensor", "matmul", psA[:, :], w1b[ob][:, ct * 128:(ct + 1) * 128],
                       xnT[:, ct * 2048 + tt * 512: ct * 2048 + tt * 512 + 512],
                       start=(ct == 0), stop=(ct == 7), inc=("f1mm" if ct == 7 else None))
            p.wait("tensor", "wld2" + "AB"[ob])
            if idx >= 1:
                p.wait("tensor", "hev", idx)
            for ct in range(8):
                p.call("tensor", "matmul", psB[:, :], w2b[ob][:, ct * 128:(ct + 1) * 128],
                       xnT[:, ct * 2048 + tt * 512: ct * 2048 + tt * 512 + 512],
                       start=(ct == 0), stop=(ct == 7), inc=("f2mm" if ct == 7 else None))
            p.wait("scalar", "f1mm", idx + 1)
            if idx >= 2:
                p.wait("scalar", "hev", idx - 1)
            p.call("scalar", "activation", hsil[ob][:, :], psA[:, :], AF.Silu,
                   scale=scl[:, o:o + 1], inc="silev")
            p.wait("vector", "f2mm", idx + 1)
            p.wait("vector", "silev", idx + 1)
            if tt > 0:
                p.wait("vector", "f4mm", tt * 8)
            p.call("vector", "scalar_tensor_tensor", h_sb[:, o * 512:(o + 1) * 512],
                   psB[:, :], scl[:, 32 + o:33 + o], hsil[ob][:, :], ALU.mult, ALU.mult, inc="hev")

        for o2 in range(32):
            ob = o2 % 2
            idx = tt * 32 + o2
            if idx >= 2:
                p.wait("gpsimd", "f3mm", idx - 1)
            if tt > 0 and o2 < 2:
                p.wait("gpsimd", "f3bmm", tt * 32)
            p.dma("gpsimd", hb[ob][:, :].rearrange("p (a d) -> p a d", a=32),
                  h1_d.rearrange("(a p) d -> p a d", p=128)[:, :, o2 * 128:(o2 + 1) * 128], "wld3" + "AB"[ob])
            p.wait("tensor", "wld3" + "AB"[ob])
            if o2 == 0:
                p.wait("tensor", "hev", (tt + 1) * 32)
                if tt > 0:
                    p.wait("tensor", "f4ev", tt * 8)
            cb = psC[o2 % 2]
            if idx >= 2:
                p.wait("tensor", "g1ev", idx - 1)
            for ot in range(32):
                p.call("tensor", "matmul", cb[:, :], hb[ob][:, ot * 128:(ot + 1) * 128],
                       h_sb[:, ot * 512:(ot + 1) * 512],
                       start=(ot == 0), stop=(ot == 31), inc=("f3mm" if ot == 31 else None))
            p.wait("scalar", "f3mm", idx + 1)
            if tt > 0 and o2 == 0:
                p.wait("scalar", "f3bmm", tt * 32)
            p.call("scalar", "activation", g_sb[:, o2 * 512:(o2 + 1) * 512], cb[:, :], AF.Gelu,
                   scale=scl[:, 64 + o2:65 + o2], inc="g1ev")

        for o2 in range(32):
            ob = o2 % 2
            idx = tt * 32 + o2
            if idx >= 2:
                p.wait("gpsimd", "f3bmm", idx - 1)
            if o2 < 2:
                p.wait("gpsimd", "f3mm", (tt + 1) * 32)
            p.dma("gpsimd", hb[ob][:, :].rearrange("p (a d) -> p a d", a=32),
                  h2_d.rearrange("(a p) d -> p a d", p=128)[:, :, o2 * 128:(o2 + 1) * 128], "wld4" + "AB"[ob])
            p.wait("tensor", "wld4" + "AB"[ob])
            if o2 == 0:
                p.wait("tensor", "g1ev", (tt + 1) * 32)
            cb = psC[o2 % 2]
            if idx >= 2:
                p.wait("tensor", "g2ev", idx - 1)
            for ot in range(32):
                p.call("tensor", "matmul", cb[:, :], hb[ob][:, ot * 128:(ot + 1) * 128],
                       g_sb[:, ot * 512:(ot + 1) * 512],
                       start=(ot == 0), stop=(ot == 31), inc=("f3bmm" if ot == 31 else None))
            p.wait("scalar", "f3bmm", idx + 1)
            if o2 == 0:
                p.wait("scalar", "f3mm", (tt + 1) * 32)
            p.call("scalar", "activation", h_sb[:, o2 * 512:(o2 + 1) * 512], cb[:, :], AF.Gelu,
                   scale=scl[:, 96 + o2:97 + o2], inc="g2ev")

        for half in range(2):
            hidx = tt * 2 + half
            if hidx >= 1:
                p.wait("gpsimd", "f4mm", (hidx - 1) * 4 + 4)
            p.dma("gpsimd", w3b[:, :].rearrange("p (a d) -> p a d", a=32),
                  w3_d.rearrange("(a p) d -> p a d", p=128)[:, :, half * 512: half * 512 + 512], "w3ld")

            for ts4 in range(4):
                idx = hidx * 4 + ts4
                cb = psC[idx % 2]
                p.wait("tensor", "w3ld", 16 * (hidx + 1))
                if ts4 == 0 and half == 0:
                    p.wait("tensor", "g2ev", (tt + 1) * 32)
                if idx >= 2:
                    p.wait("tensor", "f4ev", idx - 1)
                for ot in range(32):
                    p.call("tensor", "matmul", cb[:, :],
                           h_sb[:, ot * 512 + ts4 * 128: ot * 512 + (ts4 + 1) * 128],
                           w3b[:, ot * 512: (ot + 1) * 512],
                           start=(ot == 0), stop=(ot == 31), inc=("f4mm" if ot == 31 else None))
                p.wait("vector", "f4mm", idx + 1)
                if idx == 0:
                    p.wait("vector", "w3sbcr", p.cnt("w3sbcr"))
                p.call("vector", "tensor_mul",
                       ffn_sb[:, (tt * 4 + ts4) * 1024 + half * 512: (tt * 4 + ts4) * 1024 + half * 512 + 512],
                       cb[:, :], w3sb[:, half * 512: half * 512 + 512], inc="f4ev")

    p.wait("sync", "f4ev", 32)
    p.dma("sync", ffn_d, ffn_sb[:, :], "sto")
    p.wait("sync", "sto")
    return p


def build_LC(skip_passA=False, serial_gather=False, debug_dump=False, full_serial=False, nbuf=2, p=None):
    if p is None:
        p = Prog("LC")
    q_d = p.din("q0T", [128, 16384], BF16, "_q0T").ap()
    if p.fused:
        kvall = p.io_tensors["kv0all"]
    else:
        k_d = p.din("k0f", [B * D, T], BF16, "_k0f").ap()
        v_d = p.din("v0f", [B * T, D], BF16, "_v0f").ap()
    am_d = p.din("am01", [2 * 128, 512], F32, "_am01").ap()
    m01_d = p.din("m01", [8 * 128, 128], BF16, "_m01").ap()
    y_d = p.dout("y0T", [128, 16384], BF16).ap()

    ident, ones1 = p.init_consts()
    qT = p.sb("qT", [128, 16384], BF16)
    yT = p.sb("yT", [128, 16384], BF16)
    am = p.sb("amsb", [128, 1024], F32)
    m01 = p.sb("m01sb", [128, 1024], BF16)
    ke = [p.sb(f"ke{i}", [65, T], BF16) for i in range(nbuf)]
    vb = [p.sb(f"vb{i}", [128, 64 * 65], BF16) for i in range(nbuf)]
    qe = [p.sb("qe0", [65, 1024], BF16), p.sb("qe1", [65, 1024], BF16)]
    cm = p.sb("cm", [128, 18], F32)
    negm = p.sb("negm", [128, 1], F32)
    negmb = p.sb("negmb", [128, 1], BF16)
    smask = p.sb("smask", [128, 512], F32)
    pb = [p.sb("pbuf0", [128, 512], BF16), p.sb("pbuf1", [128, 512], BF16)]
    rrow = p.sb("rrow", [1, 512], F32)
    rrb = p.sb("rrb", [1, 512], BF16)
    rbc = p.sb("rbc", [64, 512], F32)
    psA = [p.ps("psA0", [128, 512]), p.ps("psA1", [128, 512])]
    psS = [p.ps("psS0", [128, 512]), p.ps("psS1", [128, 512])]
    pav = [p.ps("pav0", [65, 512]), p.ps("pav1", [65, 512])]
    psR = p.ps("psR", [1, 128])
    psB2 = p.ps("psB2", [64, 512])
    if debug_dump:
        dbgM_d = p.dout("dbgM", [32, 1024], F32).ap()
        dbgS_d = p.dout("dbgS", [64, 512], F32).ap()
        dbgM = p.sb("dbgM_sb", [1, 1024], F32)
        dbgS = p.sb("dbgS_sb", [1, 512], F32)

    p.dma("sync", qT[:, :], q_d, "cld")
    p.dma("sync", am[:, :].rearrange("p (a c) -> p a c", a=2), am_d.rearrange("(a p) c -> p a c", p=128), "cld")
    p.dma("sync", m01[:, :].rearrange("p (a c) -> p a c", a=8), m01_d.rearrange("(a p) c -> p a c", p=128), "cld")
    p.wait("vector", "cld")
    for par in range(nbuf):
        p.call("vector", "memset", ke[par][64:65, :], 1.0, inc="ones")
        p.call("vector", "memset", vb[par][:, :].rearrange("p (a c) -> p a c", a=64)[:, :, 64:65], 1.0, inc="ones")

    unit_avm_totals = []

    GDMA = 16 if p.fused else 2

    def gather(s):
        b, h = divmod(s, NH)
        par = s % nbuf
        if s >= nbuf:
            p.wait("sync", "avm", unit_avm_totals[s - nbuf])
        if not p.fused:
            p.dma("sync", ke[par][0:64, :], k_d[b * D + 64 * h: b * D + 64 * h + 64, :], "kld" + "ABCD"[par])
            p.dma("sync", vb[par][:, :].rearrange("p (a c) -> p a c", a=64)[:, :, 0:64],
                  v_d.rearrange("(bb a p) d -> bb p a d", bb=2, p=128)[b][:, :, 64 * h: 64 * h + 64], "kld" + "ABCD"[par])
        else:
            if s < nbuf:
                p.wait("sync", "__cc0", 1)
            for r in range(NCORES):
                p.dma("sync", bass.AP(ke[par], r * 128, [[T, 64], [1024, 8], [1, 128]]),
                      bass.AP(kvall, (r * 4096 + 128 * h + b) * 1024, [[2048, 64], [128, 8], [1, 128]]),
                      "kld" + "ABCD"[par])
                p.dma("sync", bass.AP(vb[par], 65 * r, [[64 * 65, 128], [8 * 65, 8], [1, 64]]),
                      bass.AP(kvall, (r * 4096 + 2048 + b * 1024) * 1024 + 64 * h, [[1024, 128], [128 * 1024, 8], [1, 64]]),
                      "kld" + "ABCD"[par])

    p.wait("sync", "ones", 2 * nbuf)
    if not serial_gather:
        for i in range(min(nbuf, B * NH)):
            gather(i)
    for s in range(B * NH):
        b, h = divmod(s, NH)
        par = s % nbuf
        qpar = s % 2
        if serial_gather:
            if s == 0:
                p.wait("sync", "ones", 2 * nbuf)
            gather(s)
        p.wait("vector", "kld" + "ABCD"[par], 16 * GDMA * (s // nbuf + 1))
        if s >= 2:
            p.wait("vector", "avm", unit_avm_totals[s - 2])
        p.call("vector", "tensor_copy", qe[qpar][0:64, :],
               qT[64 * (h % 2): 64 * (h % 2) + 64, (h // 2) * 2048 + b * 1024: (h // 2) * 2048 + b * 1024 + 1024],
               inc="qas")

        if skip_passA:
            p.wait("vector", "qas", s + 1)
            for j in range(NJ):
                p.call("vector", "memset", qe[qpar][64:65, j * 128:(j + 1) * 128], 0.0, inc="mrow")
        for j in (() if skip_passA else range(NJ)):
            nch = 2 * j + 2
            for ch in range(nch):
                bk = ch % 2
                if s == 0 and j == 0 and ch == 0:
                    p.wait("tensor", "ones", 2 * nbuf)
                    p.wait("tensor", "qas", 1)
                elif ch == 0 and j == 0:
                    p.wait("tensor", "qas", s + 1)
                p.wait("tensor", "pAr", max(0, p.cnt("pAr") - 1) if ch >= 2 else p.cnt("pAr"))
                p.call("tensor", "matmul", psA[bk][:, :], qe[qpar][0:64, j * 128:(j + 1) * 128],
                       ke[par][0:64, ch * 512:(ch + 1) * 512], start=True, stop=True, inc="pAm")
                p.wait("vector", "pAm")
                if ch >= 2 * j:
                    p.wait("vector", "pAr", p.cnt("pAr"))
                    p.call("vector", "tensor_add", smask[:, :], psA[bk][:, :],
                           am[:, (ch - 2 * j) * 512:(ch - 2 * j + 1) * 512], inc="vch")
                    p.wait("vector", "vch")
                    p.call("vector", "reduce_max", cm[:, ch:ch + 1], smask[:, :], axis=AX.X, inc="pAr")
                else:
                    p.call("vector", "reduce_max", cm[:, ch:ch + 1], psA[bk][:, :], axis=AX.X, inc="pAr")
            p.wait("vector", "pAr")
            p.call("vector", "reduce_max", negm[:, 0:1], cm[:, 0:nch], axis=AX.X, negate=True, inc="vch")
            p.wait("vector", "vch")
            p.call("vector", "tensor_copy", negmb[:, 0:1], negm[:, 0:1], inc="nmb")
            p.wait("tensor", "nmb")
            if p.cnt("nmb") >= 2:
                p.wait("tensor", "mrow", p.cnt("nmb") - 1)
            p.call("tensor", "matmul", psR[:, :], negmb[:, 0:1], ident[:, :],
                   start=True, stop=True, inc="mRm")
            p.wait("vector", "mRm")
            p.call("vector", "tensor_copy", qe[qpar][64:65, j * 128:(j + 1) * 128], psR[:, :], inc="mrow")

        if debug_dump:
            p.wait("vector", "mrow", 8 * (s + 1))
            p.call("vector", "tensor_copy", dbgM[:, :], qe[qpar][64:65, 0:1024], inc="vch")
            p.wait("vector", "vch")
            p.wait("sync", "vch", p.cnt("vch"))
            p.dma("sync", dbgM_d[s:s + 1, :], dbgM[:, :], "dbst")
        for a in range(2):
            ktmax = 32 * a + 32
            first = True
            for kt in range(ktmax):
                j_lo = max(4 * a, kt // 8)
                w = (4 * a + 4 - j_lo) * 128
                bk = kt % 2
                if first:
                    p.wait("tensor", "mrow", 8 * (s + 1))
                    if s >= 1:
                        p.wait("tensor", "avev", 2 * s)
                if full_serial:
                    p.wait("tensor", "avm", p.cnt("avm"))
                    p.wait("tensor", "pBe", p.cnt("pBe"))
                p.wait("tensor", "pBe", max(0, p.cnt("pBe") - 1) if kt >= 2 else p.cnt("pBe"))
                p.call("tensor", "matmul", psS[bk][:, 0:w],
                       ke[par][0:65, kt * 128:(kt + 1) * 128],
                       qe[qpar][0:65, j_lo * 128:(4 * a + 4) * 128],
                       start=True, stop=True, inc="pBm")
                pk = pb[kt % 2]
                p.wait("scalar", "pBm")
                p.wait("scalar", "avm", max(0, p.cnt("avm") - 1))
                p.call("scalar", "activation", pk[:, 0:w], psS[bk][:, 0:w], AF.Exp, inc="pBe")
                sub = kt // 8
                masked = (4 * a <= sub < 4 * a + 4)
                if masked:
                    rho = kt % 8
                    p.wait("vector", "pBe")
                    p.call("vector", "tensor_mul", pk[:, (sub - j_lo) * 128:(sub - j_lo + 1) * 128],
                           pk[:, (sub - j_lo) * 128:(sub - j_lo + 1) * 128],
                           m01[:, rho * 128:(rho + 1) * 128], inc="pBz")
                    p.wait("tensor", "pBz")
                else:
                    p.wait("tensor", "pBe")
                p.call("tensor", "matmul", pav[a][0:65, (j_lo - 4 * a) * 128:(j_lo - 4 * a) * 128 + w],
                       vb[par][:, kt * 65:(kt + 1) * 65], pk[:, 0:w],
                       start=first, stop=(kt == ktmax - 1), inc="avm")
                first = False
            p.wait("vector", "avm")
            if debug_dump:
                p.call("vector", "tensor_copy", dbgS[:, :], pav[a][64:65, :], inc="vch")
                p.wait("vector", "vch")
                p.wait("sync", "vch", p.cnt("vch"))
                p.dma("sync", dbgS_d[2 * s + a: 2 * s + a + 1, :], dbgS[:, :], "dbst")
            p.call("vector", "reciprocal", rrow[:, :], pav[a][64:65, :], inc="vch")
            p.wait("vector", "vch")
            p.call("vector", "tensor_copy", rrb[:, :], rrow[:, :], inc="rrbd")
            p.wait("tensor", "rrbd")
            if p.cnt("rrbd") >= 2:
                p.wait("tensor", "rbcd", p.cnt("rrbd") - 1)
            p.call("tensor", "matmul", psB2[:, :], ones1[0:1, 0:64], rrb[:, :],
                   start=True, stop=True, inc="rbcm")
            p.wait("vector", "rbcm")
            p.call("vector", "tensor_copy", rbc[:, :], psB2[:, :], inc="rbcd")
            p.wait("vector", "rbcd")
            dst = yT[64 * (h % 2): 64 * (h % 2) + 64,
                     (h // 2) * 2048 + b * 1024 + a * 512: (h // 2) * 2048 + b * 1024 + a * 512 + 512]
            p.call("vector", "tensor_mul", dst, pav[a][0:64, :], rbc[:, :], inc="avev")
        unit_avm_totals.append(p.cnt("avm"))
        if (not serial_gather) and s + nbuf < B * NH:
            gather(s + nbuf)

    p.wait("sync", "avev", 2 * B * NH)
    p.dma("sync", y_d, yT[:, :], "sto")
    p.wait("sync", "sto")
    return p


def build_LD(p=None):
    if p is None:
        p = Prog("LD")
    x0_d = p.din("x0s", [NTILE, 128, D], F32, "x0s").ap()
    y_d = p.din("y0T", [128, 16384], BF16, "_y0T").ap()
    f_d = p.din("ffn0", [128, 16384], BF16, "_ffn0").ap()
    wp_d = p.din("wprojT", [D, D], F32, "wprojT").ap()
    bp_d = p.din("bproj_r", [1, D], F32, "bproj_r").ap()
    x1_d = p.dout("x1", [NTILE, 128, D], F32).ap()

    ident, ones1 = p.init_consts()
    yT = p.sb("yT", [128, 16384], BF16)
    fsb = p.sb("fsb", [128, 16384], BF16)
    wp_sb = p.sb("wp_sb", [128, 8192], BF16)
    bp_row = p.sb("bp_row", [1, D], F32)
    bpb = p.sb("bpb", [128, D], F32)
    x0t = [p.sb("x0t0", [128, 1024], F32), p.sb("x0t1", [128, 1024], F32)]
    x1t = [p.sb("x1t0", [128, 1024], F32), p.sb("x1t1", [128, 1024], F32)]
    t1 = p.sb("t1s", [128, 512], F32)
    mmb = [p.ps("mmb0", [128, 512]), p.ps("mmb1", [128, 512])]
    pbc = p.ps("pbc", [128, 512])

    p.dma("sync", yT[:, :], y_d, "cld")
    p.dma("sync", fsb[:, :], f_d, "cld")
    p.dma("sync", bp_row[:, :], bp_d, "cld")
    cld_ld = p.cnt("cld")
    p.dma("gpsimd", wp_sb[:, :].rearrange("p (a d) -> p a d", a=8),
          wp_d.rearrange("(a p) d -> p a d", p=128), "wld")
    p.wait("vector", "cld")
    emit_bcast_row(p, "bp", bp_row[:, :], bpb, ones1, D, pbc)

    for i in range(NTILE):
        bx = i % 2
        if i >= 2:
            p.wait("sync", "x1st" + "AB"[bx], 16 * (i // 2))
        p.dma("sync", x0t[bx][:, :], x0_d[i], "xld" + "AB"[bx])
        for half in range(2):
            c = i * 2 + half
            bk = mmb[c % 2]
            if c == 0:
                p.wait("tensor", "cld", cld_ld)
                p.wait("tensor", "wld", 16)
            if c >= 2:
                p.wait("tensor", "prjev", c - 1)
            for ct in range(8):
                p.call("tensor", "matmul", bk[:, :],
                       yT[:, ct * 2048 + i * 128: ct * 2048 + (i + 1) * 128],
                       wp_sb[:, ct * 1024 + half * 512: ct * 1024 + half * 512 + 512],
                       start=(ct == 0), stop=(ct == 7), inc=("prjmm" if ct == 7 else None))
            p.wait("vector", "prjmm", c + 1)
            if c == 0:
                p.wait("vector", "bpbcr", p.cnt("bpbcr"))
            p.wait("vector", "vch")
            p.call("vector", "tensor_add", t1[:, :], bk[:, :], bpb[:, half * 512: half * 512 + 512], inc="vch")
            p.wait("vector", "vch")
            p.wait("vector", "xld" + "AB"[bx], 16 * (i // 2 + 1))
            p.call("vector", "tensor_add", t1[:, :], t1[:, :], x0t[bx][:, half * 512: half * 512 + 512], inc="vch")
            p.wait("vector", "vch")
            p.call("vector", "tensor_add", x1t[bx][:, half * 512: half * 512 + 512], t1[:, :],
                   fsb[:, i * 1024 + half * 512: i * 1024 + half * 512 + 512], inc="prjev")
        p.wait("sync", "prjev", 2 * (i + 1))
        p.dma("sync", x1_d[i], x1t[bx][:, :], "x1st" + "AB"[bx])
    p.wait("sync", "x1stA")
    p.wait("sync", "x1stB")
    return p


def build_LE1(p=None):
    if p is None:
        p = Prog("LE1")
    x1_d = p.din("x1", [NTILE, 128, D], F32, "_x1").ap()
    xn1T_d = p.dout("xn1T", [128, 16384], BF16).ap()
    xn1tm_d = p.dout("xn1tm", [128, 16384], BF16).ap()
    ident, ones1 = p.init_consts()
    xnT = p.sb("xnT", [128, 16384], BF16)
    xntm = p.sb("xntm", [128, 16384], BF16)
    emit_ln_prep(p, "p1", x1_d, xnT, ident, keep_tm=xntm)
    p.wait("sync", "p1xnev", 16)
    p.dma("sync", xn1T_d, xnT[:, :], "sto")
    p.wait("sync", "p1ktm", 16)
    p.dma("sync", xn1tm_d, xntm[:, :], "sto")
    p.wait("sync", "sto")
    return p


def build_LE2(p=None):
    if p is None:
        p = Prog("LE2")
    xn1tm_d = p.din("xn1tm", [128, 16384], BF16, "_xn1tm").ap()
    ln1w_d = p.din("ln1w_r", [1, D], F32, "ln1w_r").ap()
    ln1b_d = p.din("ln1b_r", [1, D], F32, "ln1b_r").ap()
    wq_d = p.din("wqr", [D, D], F32, "wqr").ap()
    wk_d = p.din("wkr", [D, D], F32, "wkr").ap()
    wv_d = p.din("wvr", [D, D], F32, "wvr").ap()
    wg_d = p.din("wgr", [D, D], F32, "wgr").ap()
    qp_d = p.din("qpat2", [1024, 512], F32, "_qpat2").ap()
    kp_d = p.din("kpat2", [1024, 512], F32, "_kpat2").ap()
    qrT_d = p.dout("qr1T", [128, 16384], BF16).ap()
    g1T_d = p.dout("gate1T", [128, 16384], BF16).ap()
    kv1_d = p.dout("kv1c", [4096, D], BF16).ap()

    ident, ones1 = p.init_consts()
    xn1tm = p.sb("xn1tm_sb", [128, 16384], BF16)
    xlnT = p.sb("xlnT", [128, 16384], BF16)
    qst = [p.sb("qst0", [128, 512], BF16), p.sb("qst1", [128, 512], BF16)]
    gst = [p.sb("gst0", [128, 512], BF16), p.sb("gst1", [128, 512], BF16)]
    w_sb = {k: p.sb(f"w{k}_sb", [128, 8192], BF16) for k in "qkvg"}
    qp_sb = p.sb("qp_sb", [128, 4096], F32)
    kp_sb = p.sb("kp_sb", [128, 4096], F32)
    lnw_row = p.sb("lnw_row", [1, D], F32)
    lnb_row = p.sb("lnb_row", [1, D], F32)
    lnwb = p.sb("lnwb", [128, D], F32)
    lnbb = p.sb("lnbb", [128, D], F32)
    kst = [p.sb("kst0", [128, 512], BF16), p.sb("kst1", [128, 512], BF16)]
    vst = [p.sb("vst0", [128, 512], BF16), p.sb("vst1", [128, 512], BF16)]
    mmb = [p.ps("mmb0", [128, 512]), p.ps("mmb1", [128, 512])]
    pbc = p.ps("pbc", [128, 512])

    p.dma("sync", xn1tm[:, :], xn1tm_d, "cld")
    p.dma("sync", lnw_row[:, :], ln1w_d, "cld")
    p.dma("sync", lnb_row[:, :], ln1b_d, "cld")
    p.dma("sync", qp_sb[:, :].rearrange("p (a d) -> p a d", a=8),
          qp_d.rearrange("(a p) d -> p a d", p=128), "cld")
    p.dma("sync", kp_sb[:, :].rearrange("p (a d) -> p a d", a=8),
          kp_d.rearrange("(a p) d -> p a d", p=128), "cld")
    for k, wd in (("q", wq_d), ("k", wk_d), ("v", wv_d), ("g", wg_d)):
        p.dma("gpsimd", w_sb[k][:, :].rearrange("p (a d) -> p a d", a=8),
              wd.rearrange("(a p) d -> p a d", p=128), "wld" + k)
    p.wait("vector", "cld")
    emit_bcast_row(p, "l1w", lnw_row[:, :], lnwb, ones1, D, pbc)
    emit_bcast_row(p, "l1b", lnb_row[:, :], lnbb, ones1, D, pbc)
    p.call("vector", "nop", inc="affg")

    xt_ext = [xn1tm[:, i * 1024:(i + 1) * 1024] for i in range(NTILE)]
    emit_ln_prep(p, "pl", None, xlnT, ident, affine=(lnwb, lnbb, "affg"), xt_ext=xt_ext)

    def grid(pre, wk_, n_wld, evac):
        def pairs(c):
            m, tc = divmod(c, 4)
            return [(w_sb[wk_][:, ct * 1024 + m * 128: ct * 1024 + (m + 1) * 128],
                     xlnT[:, ct * 2048 + tc * 512: ct * 2048 + tc * 512 + 512])
                    for ct in range(8)]

        def gate(c):
            if c == 0:
                p.wait("tensor", "plxnev", 16)
                p.wait("tensor", "wld" + n_wld, 16)
        emit_mm_grid(p, pre, mmb, 32, pairs, evac, gate)

    def q_evac(c, bank):
        m, tc = divmod(c, 4)
        sbuf = qst[c % 2]
        p.wait("vector", "q_mm")
        if c >= 2:
            p.wait("vector", "q_st" + "AB"[c % 2], 16 * (c // 2))
        p.call("vector", "scalar_tensor_tensor", sbuf[:, :],
               bank[:, :], 1.0, qp_sb[:, m * 512:(m + 1) * 512], ALU.mult, ALU.mult, inc="q_ev")
        p.wait("sync", "q_ev", p.cnt("q_ev"))
        p.dma("sync", qrT_d[:, m * 2048 + tc * 512: m * 2048 + tc * 512 + 512], sbuf[:, :], "q_st" + "AB"[c % 2])
    grid("q_", "q", "q", q_evac)

    def k_evac(c, bank):
        m, tc = divmod(c, 4)
        sbuf = kst[c % 2]
        p.wait("vector", "k_mm")
        if c >= 2:
            p.wait("vector", "k_st" + "AB"[c % 2], 16 * (c // 2))
        p.call("vector", "scalar_tensor_tensor", sbuf[:, :],
               bank[:, :], 1.0, kp_sb[:, m * 512:(m + 1) * 512], ALU.mult, ALU.mult, inc="k_ev")
        p.wait("sync", "k_ev", p.cnt("k_ev"))
        x_half, off = divmod(tc * 512, 1024)
        p.dma("sync", kv1_d.rearrange("(a p x) d -> a p x d", p=128, x=2)[m][:, x_half, off: off + 512],
              sbuf[:, :], "k_st" + "AB"[c % 2])
    grid("k_", "k", "k", k_evac)

    def v_pairs(c):
        t16, half = divmod(c, 2)
        return [(xlnT[:, ct * 2048 + t16 * 128: ct * 2048 + (t16 + 1) * 128],
                 w_sb["v"][:, ct * 1024 + half * 512: ct * 1024 + half * 512 + 512])
                for ct in range(8)]

    def v_gate(c):
        if c == 0:
            p.wait("tensor", "wldv", 16)

    def v_evac(c, bank):
        t16, half = divmod(c, 2)
        sbuf = vst[c % 2]
        p.wait("vector", "v_mm")
        if c >= 2:
            p.wait("vector", "v_st" + "AB"[c % 2], 16 * (c // 2))
        p.call("vector", "tensor_copy", sbuf[:, :], bank[:, :], inc="v_ev")
        p.wait("sync", "v_ev", p.cnt("v_ev"))
        p.dma("sync", kv1_d[2048 + t16 * 128: 2048 + (t16 + 1) * 128, half * 512: half * 512 + 512],
              sbuf[:, :], "v_st" + "AB"[c % 2])
    emit_mm_grid(p, "v_", mmb, 32, v_pairs, v_evac, v_gate)

    def g_evac(c, bank):
        m, tc = divmod(c, 4)
        sbuf = gst[c % 2]
        p.wait("scalar", "g_mm")
        if c >= 2:
            p.wait("scalar", "g_st" + "AB"[c % 2], 16 * (c // 2))
        p.call("scalar", "activation", sbuf[:, :], bank[:, :], AF.Silu, inc="g_ev")
        p.wait("sync", "g_ev", p.cnt("g_ev"))
        p.dma("sync", g1T_d[:, m * 2048 + tc * 512: m * 2048 + tc * 512 + 512], sbuf[:, :], "g_st" + "AB"[c % 2])
    grid("g_", "g", "g", g_evac)

    for nm in ("q_stA", "q_stB", "g_stA", "g_stB", "k_stA", "k_stB", "v_stA", "v_stB"):
        p.wait("sync", nm)
    return p


def build_LF(p=None):
    if p is None:
        p = Prog("LF")
    xn1T_d = p.din("xn1T", [128, 16384], BF16, "_xn1T").ap()
    xn1tm_d = p.din("xn1tm", [128, 16384], BF16, "_xn1tm").ap()
    wqkv_d = p.din("WqkvT", [D, 3 * D], F32, "WqkvT").ap()
    bqkv_d = p.din("bqkv_c", [128, 24], F32, "bqkv_c").ap()
    wo_d = p.din("WoT", [D, D], F32, "WoT").ap()
    bo_d = p.din("bo_r", [1, D], F32, "bo_r").ap()
    g1_d = p.din("g1_r", [1, D], F32, "g1_r").ap()
    be1_d = p.din("be1_r", [1, D], F32, "be1_r").ap()
    y1T_d = p.dout("y1T", [128, 16384], BF16).ap()
    y1tm_d = p.dout("y1tm", [128, 16384], BF16).ap()

    ident, ones1 = p.init_consts()
    onec = p.sb("onec", [128, 1], BF16)
    onec_t = p.nc.inline_tensor(np.ones((128, 1), dtype=BF), p.prefix + "onecc")
    p.dma("sync", onec[:, :], onec_t.ap(), "cld")

    xnT = p.sb("xnT", [128, 16384], BF16)
    xtmb = [p.sb("xtmb0", [128, 1024], BF16), p.sb("xtmb1", [128, 1024], BF16)]
    s1tm = p.sb("s1tm", [128, 16384], BF16)
    qkvT = p.sb("qkvT", [128, 24 * 1024], BF16)
    oT = p.sb("oT", [128, 8 * 1024], BF16)
    wqm = [p.sb("wqm0", [128, 1024], BF16), p.sb("wqm1", [128, 1024], BF16)]
    wo_sb = p.sb("wo_sb", [128, 8192], BF16)
    bqkv_sb = p.sb("bqkv_sb", [128, 24], F32)
    bo_row = p.sb("bo_row", [1, D], BF16)
    g1_row = p.sb("g1_row", [1, D], BF16)
    be1_row = p.sb("be1_row", [1, D], BF16)
    bob = p.sb("bob", [128, D], BF16)
    g1b = p.sb("g1b", [128, D], BF16)
    be1b = p.sb("be1b", [128, D], BF16)
    prod = [p.sb("prod0", [128, 512], BF16), p.sb("prod1", [128, 512], BF16)]
    scb = p.sb("scb", [1, 1024], F32)
    sms = p.sb("sms", [1, 5 * 512], F32)
    ab = p.sb("ab", [1, 1024], BF16)
    abb = [p.sb("abb0", [128, 512], BF16), p.sb("abb1", [128, 512], BF16)]
    tsc = p.sb("tsc", [128, 512], BF16)
    tsc2 = p.sb("tsc2", [128, 512], BF16)
    t1s = p.sb("t1s", [128, 512], F32)
    mmb = [p.ps("mmb0", [128, 512]), p.ps("mmb1", [128, 512])]
    ps1 = p.ps("ps1", [1, 512])
    pbc = p.ps("pbc", [128, 512])

    p.dma("sync", xnT[:, :], xn1T_d, "cld")
    p.dma("sync", bqkv_sb[:, :], bqkv_d, "cld")
    p.dma("gpsimd", bo_row[:, :], bo_d, "rld")
    p.dma("gpsimd", g1_row[:, :], g1_d, "rld")
    p.dma("gpsimd", be1_row[:, :], be1_d, "rld")
    p.dma("gpsimd", wo_sb[:, :].rearrange("p (a d) -> p a d", a=8),
          wo_d.rearrange("(a p) d -> p a d", p=128), "wldo")
    p.wait("vector", "cld")
    p.wait("vector", "rld")
    emit_bcast_row(p, "bo", None, bob, ones1, D, pbc, row_bf=bo_row[:, :])
    emit_bcast_row(p, "g1", None, g1b, ones1, D, pbc, row_bf=g1_row[:, :])
    emit_bcast_row(p, "be1", None, be1b, ones1, D, pbc, row_bf=be1_row[:, :])
    p.call("vector", "nop", inc="affg")

    for tc in range(2):
        for m in range(24):
            mi = tc * 24 + m
            if mi >= 2:
                p.wait("gpsimd", "qmm", 2 * (mi - 1))
            p.dma("gpsimd", wqm[mi % 2][:, :].rearrange("p (a d) -> p a d", a=8),
                  wqkv_d.rearrange("(a p) d -> p a d", p=128)[:, :, m * 128:(m + 1) * 128], "wq" + "AB"[mi % 2])
            for half in range(2):
                c = tc * 48 + m * 2 + half
                col = half * 1024 + tc * 512
                bk = mmb[c % 2]
                if c == 0:
                    p.wait("tensor", "cld", p.cnt("cld"))
                if c == 48:
                    p.wait("tensor", "s1ev", 16)
                if half == 0:
                    p.wait("tensor", "wq" + "AB"[mi % 2])
                if c >= 2:
                    p.wait("tensor", "qev", c - 1)
                for ct in range(8):
                    p.call("tensor", "matmul", bk[:, :],
                           wqm[mi % 2][:, ct * 128:(ct + 1) * 128],
                           xnT[:, ct * 2048 + col: ct * 2048 + col + 512],
                           start=(ct == 0), stop=(ct == 7), inc=("qmm" if ct == 7 else None))
                p.wait("vector", "qmm")
                p.call("vector", "tensor_scalar", qkvT[:, m * 1024 + half * 512: m * 1024 + half * 512 + 512],
                       bk[:, :], bqkv_sb[:, m:m + 1], None, ALU.add, inc="qev")

        for h4 in range(4):
            for i in range(2):
                for j in range(2):
                    for r in range(2):
                        pr = prod[r]
                        p.wait("vector", "qev", tc * 48 + 48)
                        p.wait("vector", "smm", p.cnt("smm"))
                        p.call("vector", "tensor_mul", pr[:, :],
                               qkvT[:, (2 * h4 + r) * 1024 + i * 512:(2 * h4 + r) * 1024 + i * 512 + 512],
                               qkvT[:, (8 + 2 * h4 + r) * 1024 + j * 512:(8 + 2 * h4 + r) * 1024 + j * 512 + 512],
                               inc="prd")
                        p.wait("tensor", "prd")
                        if r == 0:
                            p.wait("tensor", "scev", p.cnt("scev"))
                        p.call("tensor", "matmul", ps1[:, :], onec[:, 0:1], pr[:, :],
                               start=(r == 0), stop=(r == 1), inc=("smm" if r == 1 else None))
                    p.wait("vector", "smm")
                    p.call("vector", "tensor_scalar", scb[:, j * 512:(j + 1) * 512],
                           ps1[:, :], 1.0 / 16.0, None, ALU.mult, inc="scev")
                s0 = scb[:, 0:512]
                s1_ = scb[:, 512:1024]
                mx = sms[:, 0:512]
                e0 = sms[:, 512:1024]
                e1 = sms[:, 1024:1536]
                sm = sms[:, 1536:2048]
                rr = sms[:, 2048:2560]
                p.wait("vector", "scev", p.cnt("scev"))
                p.call("vector", "tensor_max", mx, s0, s1_, inc="vch")
                p.wait("vector", "vch")
                p.call("vector", "tensor_sub", e0, s0, mx, inc="vch")
                p.wait("vector", "vch")
                p.call("vector", "tensor_sub", e1, s1_, mx, inc="sxp")
                p.wait("scalar", "sxp")
                p.call("scalar", "activation", e0, e0, AF.Exp)
                p.call("scalar", "activation", e1, e1, AF.Exp, inc="sxd")
                p.wait("vector", "sxd")
                p.call("vector", "tensor_add", sm, e0, e1, inc="vch")
                p.wait("vector", "vch")
                p.call("vector", "reciprocal", rr, sm, inc="vch")
                p.wait("vector", "vch")
                p.call("vector", "tensor_mul", ab[:, 0:512], e0, rr, inc="vch")
                p.wait("vector", "vch")
                p.call("vector", "tensor_mul", ab[:, 512:1024], e1, rr, inc="attw")
                for j in range(2):
                    p.wait("tensor", "attw", p.cnt("attw"))
                    p.wait("tensor", "abev", p.cnt("abev"))
                    p.call("tensor", "matmul", pbc[:, :], ones1[0:1, :],
                           ab[:, j * 512:(j + 1) * 512], start=True, stop=True, inc="abm")
                    p.wait("vector", "abm")
                    p.call("vector", "tensor_copy", abb[j][:, :], pbc[:, :], inc="abev")
                for r in range(2):
                    mch = 2 * h4 + r
                    p.wait("vector", "abev", p.cnt("abev"))
                    if tc == 1 and h4 == 0 and i == 0 and r == 0:
                        p.wait("vector", "s1mm", 16)
                    p.wait("vector", "vch")
                    p.wait("vector", "oev", p.cnt("oev"))
                    p.call("vector", "tensor_mul", tsc[:, :],
                           qkvT[:, (16 + mch) * 1024 + 0 * 512: (16 + mch) * 1024 + 512], abb[0][:, :], inc="vch")
                    p.wait("vector", "vch")
                    p.call("vector", "tensor_mul", tsc2[:, :],
                           qkvT[:, (16 + mch) * 1024 + 512: (16 + mch) * 1024 + 1024], abb[1][:, :], inc="vch")
                    p.wait("vector", "vch")
                    p.call("vector", "tensor_add", oT[:, mch * 1024 + i * 512: mch * 1024 + i * 512 + 512],
                           tsc[:, :], tsc2[:, :], inc="oev")

        for ts4 in range(4):
            for ihalf in range(2):
                for dhalf in range(2):
                    c = ((tc * 4 + ts4) * 2 + ihalf) * 2 + dhalf
                    bk = mmb[c % 2]
                    ltok = ihalf * 1024 + tc * 512 + ts4 * 128
                    p.wait("tensor", "oev", p.cnt("oev"))
                    if c == 0:
                        p.wait("tensor", "wldo", 16)
                    if c >= 2:
                        p.wait("tensor", "s1ev", c - 1)
                    for ct in range(8):
                        p.call("tensor", "matmul", bk[:, :],
                               oT[:, ct * 1024 + ihalf * 512 + ts4 * 128: ct * 1024 + ihalf * 512 + (ts4 + 1) * 128],
                               wo_sb[:, ct * 1024 + dhalf * 512: ct * 1024 + dhalf * 512 + 512],
                               start=(ct == 0), stop=(ct == 7), inc=("s1mm" if ct == 7 else None))
                    lt = ltok // 128
                    if dhalf == 0:
                        xl = (p.cnt("xtlA") + p.cnt("xtlB")) // 16
                        if xl >= 2:
                            p.wait("sync", "s1ev", 2 * (xl - 1))
                        p.dma("sync", xtmb[xl % 2][:, :], xn1tm_d[:, lt * 1024:(lt + 1) * 1024], "xtl" + "AB"[xl % 2])
                    xl_cur = ((p.cnt("xtlA") + p.cnt("xtlB")) // 16 - 1) % 2
                    p.wait("vector", "s1mm")
                    if c == 0:
                        p.wait("vector", "bobcr", p.cnt("bobcr"))
                    p.wait("vector", "vch")
                    p.wait("vector", "s1ev", p.cnt("s1ev"))
                    p.call("vector", "tensor_add", t1s[:, :], bk[:, :], bob[:, dhalf * 512: dhalf * 512 + 512], inc="vch")
                    p.wait("vector", "vch")
                    p.wait("vector", "xtl" + "AB"[xl_cur])
                    p.call("vector", "tensor_add",
                           s1tm[:, lt * 1024 + dhalf * 512: lt * 1024 + dhalf * 512 + 512],
                           t1s[:, :], xtmb[xl_cur][:, dhalf * 512: dhalf * 512 + 512], inc="s1ev")

    p.wait("scalar", "s1ev")
    p.wait("vector", "s1ev")
    p.wait("tensor", "s1ev")
    xt_ext = [s1tm[:, i * 1024:(i + 1) * 1024] for i in range(NTILE)]
    emit_ln_prep(p, "py", None, None, ident, affine=(g1b, be1b, "affg"), xt_ext=xt_ext,
                 tm_dma=y1tm_d, xnT_dma=y1T_d)
    for nm in ("pyktmA", "pyktmB", "pyxnstA", "pyxnstB"):
        p.wait("sync", nm)
    return p


def build_LG(p=None):
    if p is None:
        p = Prog("LG")
    q_d = p.din("qr1T", [128, 16384], BF16, "_qr1T").ap()
    if p.fused:
        kvall = p.io_tensors["kv1all"]
    else:
        k_d = p.din("kr1f", [B * D, T], BF16, "_kr1f").ap()
        v_d = p.din("vr1f", [B * T, D], BF16, "_vr1f").ap()
    g_d = p.din("gate1T", [128, 16384], BF16, "_gate1T").ap()
    m01_d = p.din("m01", [8 * 128, 128], BF16, "_m01").ap()
    y_d = p.dout("yr1T", [128, 16384], BF16).ap()

    ident, ones1 = p.init_consts()
    onec = p.sb("onec", [128, 1], BF16)
    onec_t = p.nc.inline_tensor(np.ones((128, 1), dtype=BF), p.prefix + "onecc")
    eps6 = p.sb("eps6", [1, 1], F32)
    eps6_t = p.nc.inline_tensor(np.full((1, 1), 1e-6, np.float32), p.prefix + "eps6c")
    p.dma("sync", onec[:, :], onec_t.ap(), "cld")
    p.dma("sync", eps6[:, :], eps6_t.ap(), "cld")

    qT = p.sb("qT", [128, 16384], BF16)
    gT = p.sb("gTsb", [128, 16384], BF16)
    yT = p.sb("yT", [128, 16384], BF16)
    m01 = p.sb("m01sb", [128, 1024], BF16)
    ke = [p.sb("ke0", [64, T], BF16), p.sb("ke1", [64, T], BF16)]
    vb = [p.sb("vb0", [128, 64 * 64], BF16), p.sb("vb1", [128, 64 * 64], BF16)]
    qe = [p.sb("qe0", [64, 1024], BF16), p.sb("qe1", [64, 1024], BF16)]
    pb = [p.sb("pbuf0", [128, 512], BF16), p.sb("pbuf1", [128, 512], BF16)]
    ob = p.sb("ob", [64, 512], BF16)
    sq = p.sb("sq", [64, 512], BF16)
    rows = p.sb("rows", [1, 8 * 512], F32)
    rowb = p.sb("rowb", [1, 1024], BF16)
    r1b = p.sb("r1b", [64, 512], F32)
    r2b = p.sb("r2b", [64, 512], F32)
    tges = p.sb("tges", [64, 512], BF16)
    gstage = p.sb("gstage", [64, 512], BF16)
    psS = [p.ps("psS0", [128, 512]), p.ps("psS1", [128, 512])]
    pav = [p.ps("pav0", [64, 512]), p.ps("pav1", [64, 512])]
    pst = p.ps("pst", [1, 512])
    pst2 = p.ps("pst2", [1, 512])
    pbc2 = p.ps("pbc2", [64, 512])

    p.dma("sync", qT[:, :], q_d, "cld")
    p.dma("sync", gT[:, :], g_d, "cld")
    p.dma("sync", m01[:, :].rearrange("p (a c) -> p a c", a=8), m01_d.rearrange("(a p) c -> p a c", p=128), "cld")

    unit_avm = []

    GDMA = 16 if p.fused else 2

    def gather(s):
        b, h = divmod(s, NH)
        par = s % 2
        if s >= 2:
            p.wait("sync", "avm", unit_avm[s - 2])
        if not p.fused:
            p.dma("sync", ke[par][0:64, :], k_d[b * D + 64 * h: b * D + 64 * h + 64, :], "kld" + "AB"[par])
            p.dma("sync", vb[par][:, :].rearrange("p (a c) -> p a c", a=64),
                  v_d.rearrange("(bb a p) d -> bb p a d", bb=2, p=128)[b][:, :, 64 * h: 64 * h + 64], "kld" + "AB"[par])
        else:
            if s < 2:
                p.wait("sync", "__cc1", 1)
            for r in range(NCORES):
                p.dma("sync", bass.AP(ke[par], r * 128, [[T, 64], [1024, 8], [1, 128]]),
                      bass.AP(kvall, (r * 4096 + 128 * h + b) * 1024, [[2048, 64], [128, 8], [1, 128]]),
                      "kld" + "AB"[par])
                p.dma("sync", bass.AP(vb[par], 64 * r, [[64 * 64, 128], [8 * 64, 8], [1, 64]]),
                      bass.AP(kvall, (r * 4096 + 2048 + b * 1024) * 1024 + 64 * h, [[1024, 128], [128 * 1024, 8], [1, 64]]),
                      "kld" + "AB"[par])

    gather(0)
    gather(1)
    p.wait("vector", "cld")
    LN2 = float(np.log(2.0))
    for s in range(B * NH):
        b, h = divmod(s, NH)
        par = s % 2
        lg = float(np.log(GAMMA[h]))
        W0 = 20
        p.wait("vector", "kld" + "AB"[par], 16 * GDMA * (s // 2 + 1))
        if s >= 2:
            p.wait("vector", "avm", unit_avm[s - 2])
        p.call("vector", "tensor_copy", qe[par][0:64, :],
               qT[64 * (h % 2): 64 * (h % 2) + 64, (h // 2) * 2048 + b * 1024: (h // 2) * 2048 + b * 1024 + 1024],
               inc="qas")

        for a in range(2):
            first_av = True
            for kt in range(32 * a + 32):
                j_lo = max(4 * a, kt // 8)
                if h == 0:
                    j_hi = min(4 * a + 4, (kt + W0) // 8 + 1)
                    if j_hi <= j_lo:
                        continue
                else:
                    j_hi = 4 * a + 4
                w = (j_hi - j_lo) * 128
                bk = kt % 2
                if first_av:
                    p.wait("tensor", "qas", s + 1)
                    if s >= 2:
                        p.wait("tensor", "avev", 2 * s)
                    if s == 0 and a == 0:
                        p.wait("tensor", "cld", p.cnt("cld"))
                p.wait("tensor", "pBe", max(0, p.cnt("pBe") - 1) if p.cnt("pBm") >= 2 else p.cnt("pBe"))
                p.call("tensor", "matmul", psS[bk][:, 0:w],
                       ke[par][0:64, kt * 128:(kt + 1) * 128],
                       qe[par][0:64, j_lo * 128: j_hi * 128],
                       start=True, stop=True, inc="pBm")
                pk = pb[kt % 2]
                p.wait("scalar", "pBm")
                p.wait("scalar", "avm", max(0, p.cnt("avm") - 1))
                if h == 0:
                    for jj in range(j_lo, j_hi):
                        imm = float(np.exp(min(88.0, 128.0 * (8 * jj - kt) * lg)))
                        p.call("scalar", "activation", pk[:, (jj - j_lo) * 128:(jj - j_lo + 1) * 128],
                               psS[bk][:, (jj - j_lo) * 128:(jj - j_lo + 1) * 128], AF.Copy,
                               scale=imm, inc=("pBe" if jj == j_hi - 1 else None))
                else:
                    ex = 128.0 * (32 * a - kt) * lg
                    imm = float(np.exp(ex)) if ex > -700 else 0.0
                    p.call("scalar", "activation", pk[:, 0:w], psS[bk][:, 0:w], AF.Copy,
                           scale=imm, inc="pBe")
                sub = kt // 8
                masked = (j_lo <= sub < j_hi)
                if masked:
                    rho = kt % 8
                    p.wait("vector", "pBe")
                    p.call("vector", "tensor_mul", pk[:, (sub - j_lo) * 128:(sub - j_lo + 1) * 128],
                           pk[:, (sub - j_lo) * 128:(sub - j_lo + 1) * 128],
                           m01[:, rho * 128:(rho + 1) * 128], inc="pBz")
                    p.wait("tensor", "pBz")
                else:
                    p.wait("tensor", "pBe")
                last_kt = (kt == 32 * a + 31)
                if h == 0:
                    for jj in range(j_lo, j_hi):
                        p.call("tensor", "matmul",
                               pav[a][0:64, (jj - 4 * a) * 128:(jj - 4 * a + 1) * 128],
                               vb[par][:, kt * 64:(kt + 1) * 64],
                               pk[:, (jj - j_lo) * 128:(jj - j_lo + 1) * 128],
                               start=first_av, stop=(last_kt and jj == j_hi - 1),
                               inc=("avm" if jj == j_hi - 1 else None))
                        first_av = False
                else:
                    p.call("tensor", "matmul", pav[a][0:64, (j_lo - 4 * a) * 128:(j_lo - 4 * a) * 128 + w],
                           vb[par][:, kt * 64:(kt + 1) * 64], pk[:, 0:w],
                           start=first_av, stop=last_kt, inc="avm")
                first_av = False
            p.wait("scalar", "avm")
            p.wait("scalar", "avev", p.cnt("avev"))
            p.wait("scalar", "gnm", p.cnt("gnm"))
            p.call("scalar", "activation", ob[:, :], pav[a][0:64, :], AF.Copy, inc="gn1")
            p.call("scalar", "activation", sq[:, :], pav[a][0:64, :], AF.Square, inc="gn1")
            p.wait("tensor", "gn1")
            p.wait("tensor", "gnv", p.cnt("gnv"))
            p.call("tensor", "matmul", pst[:, :], onec[0:64, 0:1], ob[:, :], start=True, stop=True, inc="gnm")
            p.call("tensor", "matmul", pst2[:, :], onec[0:64, 0:1], sq[:, :], start=True, stop=True, inc="gnm")
            p.wait("vector", "gnm")
            mu = rows[:, 1024:1536]
            mu2 = rows[:, 1536:2048]
            var = rows[:, 2048:2560]
            sd = rows[:, 2560:3072]
            rstd = rows[:, 3072:3584]
            nmr = rows[:, 3584:4096]
            p.wait("vector", "vch")
            p.call("vector", "tensor_scalar", mu, pst[:, :], 1.0 / 64, None, ALU.mult, inc="vch")
            p.wait("vector", "vch")
            p.call("vector", "tensor_mul", mu2, mu, mu, inc="vch")
            p.wait("vector", "vch")
            p.call("vector", "scalar_tensor_tensor", var, pst2[:, :], 1.0 / 64, mu2, ALU.mult, ALU.subtract, inc="gnv")
            p.wait("scalar", "gnv")
            p.call("scalar", "activation", sd, var, AF.Sqrt, bias=eps6[0:1, 0:1], inc="gns")
            p.wait("vector", "gns")
            p.call("vector", "reciprocal", rstd, sd, inc="vch")
            p.wait("vector", "vch")
            p.call("vector", "scalar_tensor_tensor", nmr, mu, -1.0, rstd, ALU.mult, ALU.mult, inc="vch")
            p.wait("vector", "vch")
            p.call("vector", "tensor_copy", rowb[:, 0:512], rstd, inc="vch")
            p.wait("vector", "vch")
            p.call("vector", "tensor_copy", rowb[:, 512:1024], nmr, inc="gnr")
            p.wait("tensor", "gnr")
            p.wait("tensor", "gnb2e", p.cnt("gnb2e"))
            p.call("tensor", "matmul", pbc2[:, :], ones1[0:1, 0:64], rowb[:, 0:512], start=True, stop=True, inc="gnb1")
            p.wait("vector", "gnb1")
            p.wait("vector", "vch")
            p.call("vector", "tensor_copy", r1b[:, :], pbc2[:, :], inc="gnb1e")
            p.wait("tensor", "gnb1e")
            p.call("tensor", "matmul", pbc2[:, :], ones1[0:1, 0:64], rowb[:, 512:1024], start=True, stop=True, inc="gnb2")
            p.wait("vector", "gnb2")
            p.call("vector", "tensor_copy", r2b[:, :], pbc2[:, :], inc="gnb2e")
            p.wait("vector", "gnb2e", p.cnt("gnb2e"))
            p.wait("vector", "vch")
            p.call("vector", "tensor_mul", tges[:, :], ob[:, :], r1b[:, :], inc="vch")
            p.wait("vector", "vch")
            p.call("vector", "tensor_add", tges[:, :], tges[:, :], r2b[:, :], inc="vch")
            p.wait("vector", "vch")
            dst = yT[64 * (h % 2): 64 * (h % 2) + 64,
                     (h // 2) * 2048 + b * 1024 + a * 512: (h // 2) * 2048 + b * 1024 + a * 512 + 512]
            gsl = gT[64 * (h % 2): 64 * (h % 2) + 64,
                     (h // 2) * 2048 + b * 1024 + a * 512: (h // 2) * 2048 + b * 1024 + a * 512 + 512]
            p.wait("scalar", "avev", p.cnt("avev"))
            p.call("scalar", "activation", gstage[:, :], gsl, AF.Copy, inc="gstg")
            p.wait("vector", "gstg")
            p.call("vector", "tensor_mul", dst, tges[:, :], gstage[:, :], inc="avev")
        unit_avm.append(p.cnt("avm"))
        if s + 2 < B * NH:
            gather(s + 2)

    p.wait("sync", "avev", 2 * B * NH)
    p.dma("sync", y_d, yT[:, :], "sto")
    p.wait("sync", "sto")
    return p


def build_LH(p=None):
    if p is None:
        p = Prog("LH")
    x1_d = p.din("x1", [NTILE, 128, D], F32, "_x1").ap()
    xn1tm_d = p.din("xn1tm", [128, 16384], BF16, "_xn1tm").ap()
    yr_d = p.din("yr1T", [128, 16384], BF16, "_yr1T").ap()
    wo_d = p.din("wor", [D, D], F32, "wor").ap()
    l2w_d = p.din("ln2w_r", [1, D], F32, "ln2w_r").ap()
    l2b_d = p.din("ln2b_r", [1, D], F32, "ln2b_r").ap()
    wa_d = p.din("winA", [D, 2816], F32, "winA").ap()
    wb_d = p.din("winB", [D, 2816], F32, "winB").ap()
    ba_d = p.din("binA_c", [128, 22], F32, "binA_c").ap()
    bb_d = p.din("binB_c", [128, 22], F32, "binB_c").ap()
    wu_d = p.din("wout", [2816, D], F32, "wout").ap()
    bo_d = p.din("bout_r", [1, D], F32, "bout_r").ap()
    acc_d = p.dout("acc", [NTILE, 128, D], F32).ap()

    ident, ones1 = p.init_consts()
    yrT = p.sb("yrT", [128, 16384], BF16)
    wo_sb = p.sb("wo_sb", [128, 8192], BF16)
    xint = p.sb("xint", [128, 16384], BF16)
    xintT = p.sb("xintT", [128, 16384], BF16)
    aT = p.sb("aT", [128, 22 * 512], BF16)
    bT = p.sb("bT", [128, 22 * 512], BF16)
    wuc = [p.sb("wuc0", [128, 512], BF16), p.sb("wuc1", [128, 512], BF16)]
    xtmb = [p.sb("xtmb0", [128, 1024], BF16), p.sb("xtmb1", [128, 1024], BF16)]
    _x1t = p.sb("x1t0", [128, 1024], F32)
    x1t = [_x1t, _x1t]
    _acct = p.sb("acct0", [128, 1024], F32)
    acct = [_acct, _acct]
    bab = p.sb("bab", [128, 22], F32)
    bbb = p.sb("bbb", [128, 22], F32)
    l2w_row = p.sb("l2w_row", [1, D], BF16)
    l2b_row = p.sb("l2b_row", [1, D], BF16)
    bo_row = p.sb("bo_row", [1, D], BF16)
    l2wb = p.sb("l2wb", [128, D], BF16)
    l2bb = p.sb("l2bb", [128, D], BF16)
    bob = p.sb("bob", [128, D], BF16)
    t1s = p.sb("t1s", [128, 512], F32)
    wab = [p.sb("wab0", [128, 1024], BF16), p.sb("wab1", [128, 1024], BF16)]
    wbb = [p.sb("wbb0", [128, 1024], BF16), p.sb("wbb1", [128, 1024], BF16)]
    mmb = [p.ps("mmb0", [128, 512]), p.ps("mmb1", [128, 512])]
    mmc = [p.ps("mmc0", [128, 512]), p.ps("mmc1", [128, 512])]
    accb = mmb + mmc
    pbc = p.ps("pbc", [128, 512])

    p.dma("sync", yrT[:, :], yr_d, "cld")
    p.dma("sync", bab[:, :], ba_d, "cld")
    p.dma("sync", bbb[:, :], bb_d, "cld")
    p.dma("gpsimd", wo_sb[:, :].rearrange("p (a d) -> p a d", a=8),
          wo_d.rearrange("(a p) d -> p a d", p=128), "wldo")
    p.dma("gpsimd", l2w_row[:, :], l2w_d, "rld")
    p.dma("gpsimd", l2b_row[:, :], l2b_d, "rld")
    p.dma("gpsimd", bo_row[:, :], bo_d, "rld")
    p.wait("vector", "cld")
    p.wait("vector", "rld")
    emit_bcast_row(p, "l2w", None, l2wb, ones1, D, pbc, row_bf=l2w_row[:, :])
    emit_bcast_row(p, "l2b", None, l2bb, ones1, D, pbc, row_bf=l2b_row[:, :])
    emit_bcast_row(p, "bo", None, bob, ones1, D, pbc, row_bf=bo_row[:, :])
    p.call("vector", "nop", inc="affg")

    for ts in range(NTILE):
        xl = ts % 2
        if ts >= 2:
            p.wait("sync", "xiev", 2 * (ts - 1))
        p.dma("sync", xtmb[xl][:, :], xn1tm_d[:, ts * 1024:(ts + 1) * 1024], "xtl" + "AB"[xl])
        for half in range(2):
            c = ts * 2 + half
            bk = mmb[c % 2]
            if c == 0:
                p.wait("tensor", "cld", p.cnt("cld"))
                p.wait("tensor", "wldo", 16)
            if c >= 2:
                p.wait("tensor", "xiev", c - 1)
            for ct in range(8):
                p.call("tensor", "matmul", bk[:, :],
                       yrT[:, ct * 2048 + ts * 128: ct * 2048 + (ts + 1) * 128],
                       wo_sb[:, ct * 1024 + half * 512: ct * 1024 + half * 512 + 512],
                       start=(ct == 0), stop=(ct == 7), inc=("ximm" if ct == 7 else None))
            p.wait("vector", "ximm")
            p.wait("vector", "xtl" + "AB"[xl], 16 * (ts // 2 + 1))
            p.call("vector", "tensor_add", xint[:, ts * 1024 + half * 512: ts * 1024 + half * 512 + 512],
                   bk[:, :], xtmb[xl][:, half * 512: half * 512 + 512], inc="xiev")

    p.wait("scalar", "xiev")
    p.wait("vector", "xiev")
    p.wait("tensor", "xiev")
    xt_ext = [xint[:, i * 1024:(i + 1) * 1024] for i in range(NTILE)]
    emit_ln_prep(p, "l2", None, xintT, ident, affine=(l2wb, l2bb, "affg"), xt_ext=xt_ext)

    for tt in range(4):
        for och in range(22):
            for br, (wd, dst, bias) in enumerate(((wa_d, aT, bab), (wb_d, bT, bbb))):
                wb_ = (wab if br == 0 else wbb)[och % 2]
                idx = (tt * 22 + och) * 2 + br
                if tt + och > 0:
                    p.wait("gpsimd", "abmm", max(0, p.cnt("abmm") - 1))
                p.dma("gpsimd", wb_[:, :].rearrange("p (a d) -> p a d", a=8),
                      wd.rearrange("(a p) d -> p a d", p=128)[:, :, och * 128:(och + 1) * 128],
                      "wl" + "AB"[och % 2] + "ab"[br])
                bk = mmc[br]
                p.wait("tensor", "wl" + "AB"[och % 2] + "ab"[br])
                if idx == 0:
                    p.wait("tensor", "l2xnev", 16)
                if och == 0 and br == 0 and tt > 0:
                    p.wait("tensor", "acev", tt * 8)
                if tt * 22 + och >= 1:
                    p.wait("tensor", "abev" + "AB"[br], tt * 22 + och)
                for ct in range(8):
                    p.call("tensor", "matmul", bk[:, :],
                           wb_[:, ct * 128:(ct + 1) * 128],
                           xintT[:, ct * 2048 + tt * 512: ct * 2048 + tt * 512 + 512],
                           start=(ct == 0), stop=(ct == 7), inc=("abmm" if ct == 7 else None))
                if br == 0:
                    p.wait("vector", "abmm")
                    if tt > 0:
                        p.wait("vector", "cmul", tt * 22)
                    p.call("vector", "tensor_scalar", dst[:, och * 512:(och + 1) * 512],
                           bk[:, :], bias[:, och:och + 1], None, ALU.add, inc="abevA")
                else:
                    p.wait("scalar", "abmm")
                    if tt > 0:
                        p.wait("scalar", "cmul", tt * 22)
                    p.call("scalar", "activation", dst[:, och * 512:(och + 1) * 512],
                           bk[:, :], AF.Silu, bias=bias[:, och:och + 1], inc="abevB")
        for och in range(22):
            p.wait("vector", "abevA", tt * 22 + och + 1)
            p.wait("vector", "abevB", tt * 22 + och + 1)
            p.call("vector", "tensor_mul", aT[:, och * 512:(och + 1) * 512],
                   aT[:, och * 512:(och + 1) * 512], bT[:, och * 512:(och + 1) * 512], inc="cmul")

        for half in range(2):
            hidx = tt * 2 + half
            for och in range(22):
                ob_ = och % 2
                if p.cnt("wlu" + "AB"[ob_]) >= 16:
                    p.wait("gpsimd", "acmm", max(0, p.cnt("acmm") - 1))
                p.dma("gpsimd", wuc[ob_][:, :],
                      wu_d[och * 128:(och + 1) * 128, half * 512: half * 512 + 512], "wlu" + "AB"[ob_])
                p.wait("tensor", "wlu" + "AB"[ob_])
                if och == 0:
                    if half == 0:
                        p.wait("tensor", "cmul", (tt + 1) * 22)
                    if hidx >= 1:
                        p.wait("tensor", "acev", hidx * 4)
                for ts4 in range(4):
                    p.call("tensor", "matmul", accb[ts4][:, :],
                           aT[:, och * 512 + ts4 * 128: och * 512 + (ts4 + 1) * 128],
                           wuc[ob_][:, :],
                           start=(och == 0), stop=(och == 21), inc=("acmm" if ts4 == 3 else None))
            for ts4 in range(4):
                i = tt * 4 + ts4
                p.wait("sync", "acstA", p.cnt("acstA"))
                p.wait("sync", "acev", p.cnt("acev"))
                p.dma("sync", x1t[0][:, :], x1_d[i], "x1lA")
                p.wait("vector", "acmm")
                if hidx == 0 and ts4 == 0:
                    p.wait("vector", "bobcr", p.cnt("bobcr"))
                p.wait("vector", "vch")
                p.wait("vector", "acev", p.cnt("acev"))
                p.call("vector", "tensor_add", t1s[:, :], accb[ts4][:, :], bob[:, half * 512: half * 512 + 512], inc="vch")
                p.wait("vector", "vch")
                p.call("vector", "tensor_add", t1s[:, :], t1s[:, :],
                       xint[:, i * 1024 + half * 512: i * 1024 + half * 512 + 512], inc="vch")
                p.wait("vector", "vch")
                p.wait("vector", "x1lA", p.cnt("x1lA"))
                p.wait("vector", "acstA", p.cnt("acstA"))
                p.call("vector", "tensor_add", acct[0][:, half * 512: half * 512 + 512], t1s[:, :],
                       x1t[0][:, half * 512: half * 512 + 512], inc="acev")
                p.wait("sync", "acev", p.cnt("acev"))
                p.dma("sync", acc_d[i][:, half * 512: half * 512 + 512],
                      acct[0][:, half * 512: half * 512 + 512], "acstA")
    p.wait("sync", "acstA")
    return p


def build_LI(p=None):
    if p is None:
        p = Prog("LI")
    acc_d = p.din("acc", [NTILE, 128, D], F32, "_acc").ap()
    ytm_d = p.din("y1tm", [128, 16384], BF16, "_y1tm").ap()
    yT_d = p.din("y1T", [128, 16384], BF16, "_y1T").ap()
    wg_d = p.din("WgT", [D, 8], F32, "WgT").ap()
    w1_d = p.din("W1", [8 * D, DHID], F32, "_W1").ap()
    b1_d = p.din("b1_c", [8 * 128, 32], F32, "_b1c").ap()
    w2_d = p.din("W2", [8 * DHID, D], F32, "_W2").ap()
    b2_d = p.din("b2_r", [8, D], F32, "_b2r").ap()
    g2_d = p.din("g2_r", [1, D], F32, "g2_r").ap()
    be2_d = p.din("be2_r", [1, D], F32, "be2_r").ap()
    y_d = p.dout("yout", [NTILE, 128, D], F32, final=True).ap()

    ident, ones1 = p.init_consts()
    yT = p.sb("yTsb", [128, 16384], BF16)
    ytm = p.sb("ytm", [128, 16384], BF16)
    macc = p.sb("macc", [128, 16384], BF16)
    heT = p.sb("heT", [128, 32 * 512], BF16)
    wg_sb = p.sb("wg_sb", [128, 64], BF16)
    b1_sb = p.sb("b1_sb", [128, 8 * 32], F32)
    b2row = p.sb("b2row", [1, D], BF16)
    b2b = p.sb("b2b", [128, D], F32)
    g2row = p.sb("g2row", [1, D], BF16)
    be2row = p.sb("be2row", [1, D], BF16)
    g2b = p.sb("g2b", [128, D], BF16)
    be2b = p.sb("be2b", [128, D], BF16)
    lgb = p.sb("lgb", [128, 8], F32)
    mxb = p.sb("mxb", [128, 8], F32)
    meb = p.sb("meb", [128, 16 * 8], F32)
    w1b = [p.sb("w1b0", [128, 1024], BF16), p.sb("w1b1", [128, 1024], BF16)]
    w2b = [p.sb("w2b0", [128, 1024], BF16), p.sb("w2b1", [128, 1024], BF16)]
    acct = p.sb("acct", [128, 1024], F32)
    zin = p.sb("zin", [128, 1024], F32)
    zsq = p.sb("zsq", [128, 1024], F32)
    st = p.sb("stcols", [128, 8], F32)
    outt = p.sb("outt", [128, 1024], F32)
    t1s = p.sb("t1s", [128, 512], F32)
    mmb = [p.ps("mmb0", [128, 512]), p.ps("mmb1", [128, 512])]
    pbc = p.ps("pbc", [128, 512])
    accb = [p.ps(f"accp{i}", [128, 512]) for i in range(5)] + [mmb[0], mmb[1], pbc]

    p.dma("sync", yT[:, :], yT_d, "cld")
    p.dma("sync", ytm[:, :], ytm_d, "cld")
    p.dma("sync", b1_sb[:, :].rearrange("p (a c) -> p a c", a=8), b1_d.rearrange("(a p) c -> p a c", p=128), "cld")
    p.dma("gpsimd", wg_sb[:, 0:8].rearrange("p (a c) -> p a c", a=1), wg_d[0:128, :], "wldg")
    for ct in range(1, 8):
        p.dma("gpsimd", wg_sb[:, ct * 8:(ct + 1) * 8], wg_d[ct * 128:(ct + 1) * 128, :], "wldg")
    p.dma("gpsimd", g2row[:, :], g2_d, "rld")
    p.dma("gpsimd", be2row[:, :], be2_d, "rld")
    p.wait("vector", "cld")
    p.wait("vector", "rld")
    emit_bcast_row(p, "g2", None, g2b, ones1, D, pbc, row_bf=g2row[:, :])
    emit_bcast_row(p, "be2", None, be2b, ones1, D, pbc, row_bf=be2row[:, :])
    p.call("vector", "memset", macc[:, :], 0.0, inc="mz")

    for ts in range(NTILE):
        bk = mmb[ts % 2]
        if ts == 0:
            p.wait("tensor", "wldg", 16 * 8)
            p.wait("tensor", "cld", p.cnt("cld"))
        if ts >= 2:
            p.wait("tensor", "lgev", ts - 1)
        for ct in range(8):
            p.call("tensor", "matmul", bk[:, 0:8],
                   yT[:, ct * 2048 + ts * 128: ct * 2048 + (ts + 1) * 128],
                   wg_sb[:, ct * 8:(ct + 1) * 8],
                   start=(ct == 0), stop=(ct == 7), inc=("lgmm" if ct == 7 else None))
        p.wait("vector", "lgmm")
        p.wait("vector", "vch")
        p.call("vector", "tensor_copy", lgb[:, :], bk[:, 0:8], inc="lgev")
        p.wait("vector", "lgev")
        p.call("vector", "reduce_max", mxb[:, 0:1], lgb[:, :], axis=AX.X, inc="vch")
        p.wait("vector", "vch")
        p.call("vector", "memset", mxb[:, 2:3], 1.0, inc="vch")
        for e in range(8):
            p.wait("vector", "vch")
            p.call("vector", "tensor_scalar", mxb[:, 1:2], lgb[:, e:e + 1], mxb[:, 0:1], None, ALU.is_equal, inc="vch")
            p.wait("vector", "vch")
            p.call("vector", "tensor_mul", meb[:, ts * 8 + e: ts * 8 + e + 1], mxb[:, 1:2], mxb[:, 2:3], inc="vch")
            p.wait("vector", "vch")
            p.call("vector", "tensor_sub", mxb[:, 2:3], mxb[:, 2:3], meb[:, ts * 8 + e: ts * 8 + e + 1], inc="vch")
        p.wait("vector", "vch")
        p.call("vector", "nop", inc="msk")

    for e in range(8):
        if e > 0:
            p.wait("gpsimd", "maev", p.cnt("maev"))
        p.dma("gpsimd", b2row[:, :], b2_d[e:e + 1, :], "b2rl")
        p.wait("tensor", "b2rl", 16 * (e + 1))
        p.wait("tensor", "b2be", p.cnt("b2be"))
        if e > 0:
            p.wait("tensor", "maev", p.cnt("maev"))
        p.call("tensor", "matmul", pbc[:, :], ones1[0:1, :], b2row[:, 0:512], start=True, stop=True, inc="b2m")
        p.wait("vector", "b2m")
        p.call("vector", "tensor_copy", b2b[:, 0:512], pbc[:, :], inc="b2be")
        p.wait("tensor", "b2be")
        p.call("tensor", "matmul", pbc[:, :], ones1[0:1, :], b2row[:, 512:1024], start=True, stop=True, inc="b2m")
        p.wait("vector", "b2m")
        p.call("vector", "tensor_copy", b2b[:, 512:1024], pbc[:, :], inc="b2be")
        for tt in range(4):
            for o in range(32):
                ob = o % 2
                idx = (e * 4 + tt) * 32 + o
                if idx >= 2:
                    p.wait("gpsimd", "hemm", idx - 1)
                p.dma("gpsimd", w1b[ob][:, :].rearrange("p (a d) -> p a d", a=8),
                      w1_d.rearrange("(a p) d -> p a d", p=128)[:, e * 8:(e + 1) * 8, o * 128:(o + 1) * 128],
                      "wl1" + "AB"[ob])
                bk = mmb[o % 2]
                p.wait("tensor", "wl1" + "AB"[ob])
                if idx == 0:
                    p.wait("tensor", "msk", 16)
                if o == 0:
                    p.wait("tensor", "maev", p.cnt("maev"))
                if idx >= 2:
                    p.wait("tensor", "heev", idx - 1)
                for ct in range(8):
                    p.call("tensor", "matmul", bk[:, :],
                           w1b[ob][:, ct * 128:(ct + 1) * 128],
                           yT[:, ct * 2048 + tt * 512: ct * 2048 + tt * 512 + 512],
                           start=(ct == 0), stop=(ct == 7), inc=("hemm" if ct == 7 else None))
                p.wait("scalar", "hemm")
                if idx >= 1:
                    p.wait("scalar", "acmm", p.cnt("acmm"))
                p.call("scalar", "activation", heT[:, o * 512:(o + 1) * 512], bk[:, :], AF.Relu,
                       bias=b1_sb[:, e * 32 + o: e * 32 + o + 1], inc="heev")
            for o in range(32):
                ob = o % 2
                if p.cnt("wl2" + "AB"[ob]) >= 16:
                    p.wait("gpsimd", "acmm", max(0, p.cnt("acmm") - 1))
                p.dma("gpsimd", w2b[ob][:, :],
                      w2_d[e * DHID + o * 128: e * DHID + (o + 1) * 128, :], "wl2" + "AB"[ob])
                p.wait("tensor", "wl2" + "AB"[ob])
                if o == 0:
                    p.wait("tensor", "heev", (e * 4 + tt + 1) * 32)
                    p.wait("tensor", "b2be", p.cnt("b2be"))
                    if e + tt > 0:
                        p.wait("tensor", "maev", p.cnt("maev"))
                for ts4 in range(4):
                    for half in range(2):
                        p.call("tensor", "matmul", accb[ts4 * 2 + half][:, :],
                               heT[:, o * 512 + ts4 * 128: o * 512 + (ts4 + 1) * 128],
                               w2b[ob][:, half * 512: half * 512 + 512],
                               start=(o == 0), stop=(o == 31),
                               inc=("acmm" if (ts4 == 3 and half == 1) else None))
            for ts4 in range(4):
                i = tt * 4 + ts4
                for half in range(2):
                    p.wait("vector", "acmm")
                    p.wait("vector", "vch")
                    p.wait("vector", "maev", p.cnt("maev"))
                    if e == 0 and tt == 0 and ts4 == 0 and half == 0:
                        p.wait("vector", "mz", 1)
                    p.call("vector", "tensor_add", t1s[:, :], accb[ts4 * 2 + half][:, :], b2b[:, half * 512: half * 512 + 512], inc="vch")
                    p.wait("vector", "vch")
                    p.call("vector", "scalar_tensor_tensor",
                           macc[:, i * 1024 + half * 512: i * 1024 + half * 512 + 512],
                           t1s[:, :], meb[:, i * 8 + e: i * 8 + e + 1],
                           macc[:, i * 1024 + half * 512: i * 1024 + half * 512 + 512],
                           ALU.mult, ALU.add, inc="maev")

    for i in range(NTILE):
        p.wait("vector", "maev", p.cnt("maev"))
        p.wait("vector", "zst", p.cnt("zst"))
        p.call("vector", "tensor_add", zin[:, :], ytm[:, i * 1024:(i + 1) * 1024], macc[:, i * 1024:(i + 1) * 1024], inc="vch")
        p.wait("scalar", "vch")
        p.call("scalar", "activation", zsq[:, :], zin[:, :], AF.Square, inc="sqd")
        p.wait("vector", "vch")
        p.call("vector", "reduce_sum", st[:, 0:1], zin[:, :], axis=AX.X, inc="vch")
        p.wait("vector", "sqd")
        p.call("vector", "reduce_sum", st[:, 1:2], zsq[:, :], axis=AX.X, inc="vch")
        p.wait("vector", "vch")
        p.call("vector", "tensor_scalar", st[:, 2:3], st[:, 0:1], 1.0 / D, None, ALU.mult, inc="vch")
        p.wait("vector", "vch")
        p.call("vector", "tensor_mul", st[:, 3:4], st[:, 2:3], st[:, 2:3], inc="vch")
        p.wait("vector", "vch")
        p.call("vector", "scalar_tensor_tensor", st[:, 4:5], st[:, 1:2], 1.0 / D, st[:, 3:4], ALU.mult, ALU.subtract, inc="varr")
        p.wait("scalar", "varr")
        p.call("scalar", "activation", st[:, 5:6], st[:, 4:5], AF.Sqrt, bias=p.eps_col[:, 0:1], inc="sdr")
        p.wait("vector", "sdr")
        p.call("vector", "reciprocal", st[:, 6:7], st[:, 5:6], inc="vch")
        p.wait("vector", "vch")
        p.call("vector", "tensor_scalar", zsq[:, :], zin[:, :], st[:, 2:3], st[:, 6:7], ALU.subtract, ALU.mult, inc="vch")
        p.wait("vector", "vch")
        if i == 0:
            p.wait("vector", "be2bcr", p.cnt("be2bcr"))
        p.call("vector", "tensor_mul", zsq[:, :], zsq[:, :], g2b[:, :], inc="vch")
        p.wait("vector", "vch")
        p.call("vector", "tensor_add", zsq[:, :], zsq[:, :], be2b[:, :], inc="vch")
        p.wait("sync", "zst", p.cnt("zst"))
        p.wait("sync", "outev", p.cnt("outev"))
        p.dma("sync", acct[:, :], acc_d[i], "accl")
        p.wait("vector", "vch")
        p.wait("vector", "accl")
        p.call("vector", "tensor_add", outt[:, :], zsq[:, :], acct[:, :], inc="outev")
        p.wait("sync", "outev", p.cnt("outev"))
        p.dma("sync", y_d[i], outt[:, :], "zst")
    p.wait("sync", "zst")
    return p






def emit_allgather(p, src_name, dst_name, cc):
    src = p.io_tensors[src_name]
    dst = p.nc.dram_tensor(dst_name, [NCORES * 4096, D], BF16, kind="Internal",
                           addr_space="Shared")
    p.io_tensors[dst_name] = dst
    p.call("gpsimd", "collective_compute", "AllGather", ALU.bypass,
           replica_groups=[list(range(NCORES))], ins=[src.ap()], outs=[dst.ap()],
           inc=cc, n=1)


def build_FUSED(upto=None, debug_finals=()):
    p = Prog("FUSED", fused=True)
    p.debug_finals = set(debug_finals)
    phases = (
        ("A:", lambda: build_LA(None, p=p)),
        ("B:", lambda: (emit_allgather(p, "kv0c", "kv0all", "__cc0"), build_LB(p=p))),
        ("C:", lambda: build_LC(p=p)),
        ("D:", lambda: build_LD(p=p)),
        ("E:", lambda: build_LE1(p=p)),
        ("e:", lambda: build_LE2(p=p)),
        ("F:", lambda: (emit_allgather(p, "kv1c", "kv1all", "__cc1"), build_LF(p=p))),
        ("G:", lambda: build_LG(p=p)),
        ("H:", lambda: build_LH(p=p)),
        ("I:", lambda: build_LI(p=p)),
    )
    for pre, emit in phases:
        p.phase_start(pre)
        emit()
        p.phase_end()
        if upto is not None and pre == upto:
            break
    return p


_PROGS = {}


def _get_prog(name):
    if name not in _PROGS:
        builder = {"LA": build_LA, "LB": build_LB, "LC": build_LC, "LD": build_LD,
                   "LE1": build_LE1, "LE2": build_LE2, "LF": build_LF, "LG": build_LG,
                   "LH": build_LH, "LI": build_LI, "FUSED": build_FUSED}[name]
        p = builder() if name not in ("LA",) else build_LA(None)
        p.nc_built = p.build()
        _PROGS[name] = p
    return _PROGS[name]


def _in_maps(p, sh, cores, dyn):
    in_maps = []
    for c in range(NCORES):
        m = {}
        for iname, key in p.input_keys.items():
            if key in dyn:
                v = dyn[key]
                m[iname] = v[c] if isinstance(v, list) else v
            elif key == "x0s":
                m[iname] = dyn["x0s"][c]
            elif key == "_am01":
                m[iname] = cores[c]["am01"].reshape(256, 512)
            elif key == "_m01":
                m[iname] = cores[c]["m01"].reshape(1024, 128)
            elif key == "_qpat2":
                m[iname] = cores[c]["qpat2"]
            elif key == "_kpat2":
                m[iname] = cores[c]["kpat2"]
            elif key == "_W1":
                m[iname] = sh["W1"].reshape(8 * D, DHID)
            elif key == "_b1c":
                m[iname] = sh["b1_c"].reshape(8 * 128, 32)
            elif key == "_W2":
                m[iname] = sh["W2"].reshape(8 * DHID, D)
            elif key == "_b2r":
                m[iname] = sh["b2_r"]
            else:
                m[iname] = sh[key]
        in_maps.append(m)
    return in_maps


def _run(name, sh, cores, dyn):
    p = _get_prog(name)
    in_maps = _in_maps(p, sh, cores, dyn)
    res = run_bass_kernel_spmd(p.nc_built, in_maps, core_ids=list(range(NCORES)))
    out = [{k: np.asarray(v) for k, v in r.items()} for r in res.results]
    del res
    import gc
    gc.collect()
    return out


def _assemble_kv(kvs):
    kf = np.zeros((B, D, 8, NCORES, 128), dtype=BF)
    vf = np.zeros((B, 8, NCORES, 128, D), dtype=BF)
    for c in range(NCORES):
        kc = np.asarray(kvs[c][0:2048]).reshape(D, 2, 8, 128)
        kf[:, :, :, c, :] = kc.transpose(1, 0, 2, 3)
        vc = np.asarray(kvs[c][2048:4096]).reshape(2, 8, 128, D)
        vf[:, :, c] = vc
    return (np.ascontiguousarray(kf.reshape(B * D, T)),
            np.ascontiguousarray(vf.reshape(B * T, D)))


FUSED_MODE = True


def kernel(**inputs):
    sh = stage_shared(inputs)
    cores = [stage_core(c) for c in range(NCORES)]
    x0s = [stage_x(inputs["L0_attn_x"], c) for c in range(NCORES)]

    if FUSED_MODE:
        p = _get_prog("FUSED")
        in_maps = _in_maps(p, sh, cores, {"x0s": x0s})
        res = run_bass_kernel_spmd(p.nc_built, in_maps, core_ids=list(range(NCORES)))
        out = [np.asarray(r["yout"]) for r in res.results]
        del res
        return unstage_y(out)

    ra = _run("LA", sh, cores, {"x0s": x0s})
    rb = _run("LB", sh, cores, {"_xn0T": [r["xn0T"] for r in ra]})
    k0f, v0f = _assemble_kv([r["kv0c"] for r in ra])
    rc = _run("LC", sh, cores, {"_q0T": [r["q0T"] for r in ra], "_k0f": k0f, "_v0f": v0f})
    rd = _run("LD", sh, cores, {"x0s": x0s, "_y0T": [r["y0T"] for r in rc],
                                "_ffn0": [r["ffn0"] for r in rb]})
    re1 = _run("LE1", sh, cores, {"_x1": [r["x1"] for r in rd]})
    re2 = _run("LE2", sh, cores, {"_xn1tm": [r["xn1tm"] for r in re1]})
    k1f, v1f = _assemble_kv([r["kv1c"] for r in re2])
    rf = _run("LF", sh, cores, {"_xn1T": [r["xn1T"] for r in re1],
                                "_xn1tm": [r["xn1tm"] for r in re1]})
    rg = _run("LG", sh, cores, {"_qr1T": [r["qr1T"] for r in re2], "_kr1f": k1f,
                                "_vr1f": v1f, "_gate1T": [r["gate1T"] for r in re2]})
    rh = _run("LH", sh, cores, {"_x1": [r["x1"] for r in rd],
                                "_xn1tm": [r["xn1tm"] for r in re1],
                                "_yr1T": [r["yr1T"] for r in rg]})
    ri = _run("LI", sh, cores, {"_acc": [r["acc"] for r in rh],
                                "_y1tm": [r["y1tm"] for r in rf],
                                "_y1T": [r["y1T"] for r in rf]})
    return unstage_y([np.asarray(r["yout"]) for r in ri])
